# version 14
# baseline (speedup 1.0000x reference)
import math
import contextlib
import numpy as np
import concourse.bass as bass
import concourse.mybir as mybir

F32 = mybir.dt.float32
BF16 = mybir.dt.bfloat16
I32 = mybir.dt.int32
AF = mybir.ActivationFunctionType
OP = mybir.AluOpType
AX = mybir.AxisListType

ENGINES = ("pe", "act", "dve", "pool", "sp")

LN2 = float(np.log(2.0))
C_LG = float(1.0 / (2.0 * np.log(2.0)))
C_HI = float(np.float32(LN2))
C_LO = float(np.float32(np.float64(LN2) - np.float64(np.float32(LN2))))
QUANT_COMPARE = True
DEBUG_ANNOTATE = False
C1_448 = float(np.float32(1.0 / 448.0))
C3_448 = float(np.float32(1.0 / 448.0 - np.float64(np.float32(1.0 / 448.0))))


class Cfg:
    def __init__(self, T=2048, D=2048, I=8192, BC=2, TT=512, IW=256,
                 QSLOTS=3):
        self.T, self.D, self.I, self.BC, self.TT, self.IW = T, D, I, BC, TT, IW
        self.NH = D // 128
        self.QSLOTS = QSLOTS
        assert D % 128 == 0 and T % TT == 0 and TT % 128 == 0
        assert I % IW == 0 and I % 512 == 0
        u = 1.0 / math.sqrt(T)
        self.u_hi = float(np.float32(u))
        self.u_lo = float(np.float32(np.float64(u) - np.float64(self.u_hi)))


class Prog:
    NLANES = 8

    def __init__(self, nc):
        self.nc = nc
        self.cur_tag = ""
        self.steps = []
        self.last_w = {}
        self.readers = {}
        self.bar_deps = []
        self.dma_count = {e: 0 for e in ENGINES}

    def step(self, engine, fn, reads=(), writes=(), inc=1, lane=None):
        idx = len(self.steps)
        deps = set(self.bar_deps)
        for bk in reads:
            if bk in self.last_w:
                deps.add(self.last_w[bk])
        for bk in writes:
            if bk in self.last_w:
                deps.add(self.last_w[bk])
            for r in self.readers.get(bk, {}).values():
                deps.add(r)
        deps.discard(idx)
        self.steps.append(dict(i=idx, engine=engine, fn=fn, inc=inc,
                               deps=deps, need=False, lane=lane,
                               tag=self.cur_tag, reads=list(reads),
                               writes=list(writes)))
        for bk in reads:
            self.readers.setdefault(bk, {})[engine] = idx
        for bk in writes:
            self.last_w[bk] = idx
            self.readers[bk] = {}
        return idx

    def dma(self, queue, out, in_, reads=(), writes=()):
        lane = self.dma_count[queue] % self.NLANES
        self.dma_count[queue] += 1
        return self.step(
            queue, lambda e, o=out, i=in_: e.dma_start(out=o, in_=i),
            reads=reads, writes=writes, inc=16, lane=lane)

    def barrier(self):
        last = {}
        for st in self.steps:
            last[(st["engine"], st["lane"])] = st["i"]
        self.bar_deps = list(last.values())

    relax = True

    def emit(self, block, sems):
        for st in self.steps:
            if st["lane"] is not None:
                st["need"] = True
        SELF_ORDERED = ("pe", "dve", "act") if self.relax else ("pe",)

        def implicit(ds, st):
            return (ds["engine"] == st["engine"] and st["lane"] is None
                    and ds["lane"] is None and ds["engine"] in SELF_ORDERED)

        for st in self.steps:
            for d in st["deps"]:
                ds = self.steps[d]
                if implicit(ds, st):
                    continue
                ds["need"] = True
        cum = {e: 0 for e in ENGINES}
        lcum = {}
        prev_on_lane = {}
        for st in self.steps:
            if st["lane"] is not None:
                key = (st["engine"], st["lane"])
                lcum[key] = lcum.get(key, 0) + 16
                st["cum"] = lcum[key]
                st["semkey"] = key
                st["prev_lane_cum"] = st["cum"] - 16
            else:
                if st["need"]:
                    cum[st["engine"]] += st["inc"]
                st["cum"] = cum[st["engine"]]
                st["semkey"] = st["engine"]
        for st in self.steps:
            w = {}
            for d in st["deps"]:
                ds = self.steps[d]
                if implicit(ds, st):
                    continue
                w[ds["semkey"]] = max(w.get(ds["semkey"], 0), ds["cum"])
            st["waits"] = w
        finals = dict(cum)
        finals.update(lcum)
        nst = {e: sum(1 for s in self.steps if s["engine"] == e)
               for e in ENGINES}
        print("prog steps:", {e: n for e, n in nst.items() if n},
              "sem finals:", {k: v for k, v in finals.items() if v},
              flush=True)

        def mk_runner(name):
            def run(eng):
                waited = {}
                for st in self.steps:
                    if st["engine"] != name:
                        continue
                    if st["lane"] is not None and st["prev_lane_cum"] > 0:
                        sk = st["semkey"]
                        if waited.get(sk, -1) < st["prev_lane_cum"]:
                            eng.wait_ge(sems[sk], st["prev_lane_cum"])
                            waited[sk] = st["prev_lane_cum"]
                    for sk, v in sorted(st["waits"].items(), key=str):
                        if v > 0 and waited.get(sk, -1) < v:
                            eng.wait_ge(sems[sk], v)
                            waited[sk] = v
                    inst = st["fn"](eng)
                    if DEBUG_ANNOTATE:
                        inst.annotate(f"{st['tag']}#i{st['i']}")
                    if st["need"]:
                        inst.then_inc(sems[st["semkey"]], st["inc"])
                if name == "sp":
                    for k, v in sorted(finals.items(), key=str):
                        if isinstance(k, tuple) and v > 0 and waited.get(k, -1) < v:
                            eng.wait_ge(sems[k], v)
            return run

        block.tensor(mk_runner("pe"))
        block.scalar(mk_runner("act"))
        block.vector(mk_runner("dve"))
        block.gpsimd(mk_runner("pool"))
        block.sync(mk_runner("sp"))


class Alloc:
    def __init__(self, nc, es):
        self.nc, self.es = nc, es

    def sb(self, name, shape, dtype):
        return self.es.enter_context(self.nc.sbuf_tensor(name, shape, dtype))

    def ps(self, name, shape, dtype):
        return self.es.enter_context(self.nc.psum_tensor(name, shape, dtype))


class QuantPool:
    def __init__(self, al, tag, F, slots):
        self.F = F
        self.slots = slots
        self.t = []
        self.tag = tag
        for s in range(slots):
            d = {}
            for nm in ("qy", "qsq", "ql", "qtau", "qm"):
                d[nm] = al.sb(f"{tag}_{nm}{s}", [128, F], F32)
            for nm in ("qfe", "qf"):
                d[nm] = al.sb(f"{tag}_{nm}{s}", [128, F], I32)
            for nm in ("qstep", "qrstep", "qub", "qqb"):
                d[nm] = al.sb(f"{tag}_{nm}{s}", [128, F], BF16)
            for nm in ("b0", "b1", "b2"):
                d[nm] = al.sb(f"{tag}_{nm}{s}", [128, F // 32], F32)
            d["b3"] = al.sb(f"{tag}_b3{s}", [128, F // 32], BF16)
            self.t.append(d)
        self.i = 0

    def next(self):
        s = self.i % self.slots
        self.i += 1
        d = self.t[s]
        keys = {nm: f"{self.tag}{s}_{nm}" for nm in d}
        return d, keys


def quant_stages(pr, qp, consts, t_ap, t_key, dq_ap, dq_key):
    F = qp.F
    d, K = qp.next()
    nb = F // 32
    t3 = t_ap.rearrange("p (n b) -> p n b", b=32)
    lnz, b126 = consts["lnz"], consts["b126"]
    c635, cm05 = consts["c635"], consts["cm05"]
    tgt = [None]

    def S(eng, fn, reads, writes):
        tgt[0].append(lambda eng=eng, fn=fn, reads=reads, writes=writes:
                      pr.step(eng, fn, reads=reads, writes=writes))

    bm, s0, s2, s3 = d["b0"], d["b1"], d["b2"], d["b3"]

    def stageA():
        S("dve", lambda e: e.tensor_reduce(bm[:].unsqueeze(2), t3, AX.X,
                                           OP.max, apply_absolute_value=True),
          [t_key], [K["b0"]])
        S("dve", lambda e: e.tensor_scalar(s0[:], bm[:], C1_448, 1e-30,
                                           OP.mult, OP.max),
          [K["b0"]], [K["b1"]])
        S("dve", lambda e: e.reciprocal(s2[:], s0[:]), [K["b1"]], [K["b2"]])
        S("dve", lambda e: e.tensor_copy(s3[:], s0[:]), [K["b1"]], [K["b3"]])
        rs_b = s2[:].unsqueeze(2).broadcast_to([128, nb, 32])
        qy3 = d["qy"][:].rearrange("p (n b) -> p n b", b=32)
        S("dve", lambda e: e.tensor_tensor(qy3, t3, rs_b, OP.mult),
          [t_key, K["b2"]], [K["qy"]])
        S("act", lambda e: e.activation(d["qsq"][:], d["qy"][:], AF.Square),
          [K["qy"]], [K["qsq"]])
        S("act", lambda e: e.activation(d["ql"][:], d["qsq"][:], AF.Ln,
                                        bias=lnz[:]),
          [K["qsq"], "c_lnz"], [K["ql"]])

    def stageB():
        S("act", lambda e: e.activation(d["qfe"][:], d["ql"][:], AF.Identity,
                                        bias=c635[:], scale=C_LG),
          [K["ql"], "c_c635"], [K["qfe"]])
        S("act", lambda e: e.activation(d["qtau"][:], d["qfe"][:], AF.Exp,
                                        bias=b126[:], scale=2.0 * LN2),
          [K["qfe"], "c_b126"], [K["qtau"]])
        S("dve", lambda e: e.tensor_tensor(d["ql"][:], d["qsq"][:],
                                           d["qtau"][:], OP.is_ge),
          [K["qsq"], K["qtau"]], [K["ql"]])
        S("dve", lambda e: e.scalar_tensor_tensor(d["qtau"][:], d["qfe"][:],
                                                  -67.0, d["ql"][:],
                                                  OP.add, OP.add),
          [K["qfe"], K["ql"]], [K["qtau"]])
        S("dve", lambda e: e.tensor_scalar(d["qsq"][:], d["qtau"][:], -9.0,
                                           C_HI, OP.max, OP.mult),
          [K["qtau"]], [K["qsq"]])
        S("act", lambda e: e.activation(d["qstep"][:], d["qsq"][:], AF.Exp,
                                        scale=1.0),
          [K["qsq"]], [K["qstep"]])
        S("act", lambda e: e.activation(d["qrstep"][:], d["qsq"][:], AF.Exp,
                                        scale=-1.0),
          [K["qsq"]], [K["qrstep"]])

    def stageC():
        sb_b = s3[:].unsqueeze(2).broadcast_to([128, nb, 32])
        S("dve", lambda e: e.tensor_tensor(d["qm"][:], d["qy"][:],
                                           d["qrstep"][:], OP.mult),
          [K["qy"], K["qrstep"]], [K["qm"]])
        if QUANT_COMPARE:
            S("act", lambda e: e.activation(d["qf"][:], d["qm"][:],
                                            AF.Identity, bias=cm05[:]),
              [K["qm"], "c_cm05"], [K["qf"]])
            S("dve", lambda e: e.scalar_tensor_tensor(d["qtau"][:], d["qm"][:],
                                                      -0.5, d["qf"][:],
                                                      OP.add, OP.is_gt),
              [K["qm"], K["qf"]], [K["qtau"]])
            S("pool", lambda e: e.tensor_copy(d["ql"][:], d["qf"][:]),
              [K["qf"]], [K["ql"]])
            S("pool", lambda e: e.tensor_tensor(d["qqb"][:], d["ql"][:],
                                                d["qtau"][:], OP.add),
              [K["ql"], K["qtau"]], [K["qqb"]])
            S("pool", lambda e: e.tensor_tensor(d["qub"][:], d["qqb"][:],
                                                d["qstep"][:], OP.mult),
              [K["qqb"], K["qstep"]], [K["qub"]])
        else:
            S("act", lambda e: e.activation(d["qf"][:], d["qm"][:],
                                            AF.Identity),
              [K["qm"]], [K["qf"]])
            S("pool", lambda e: e.tensor_copy(d["qqb"][:], d["qf"][:]),
              [K["qf"]], [K["qqb"]])
            S("pool", lambda e: e.tensor_tensor(d["qub"][:], d["qqb"][:],
                                                d["qstep"][:], OP.mult),
              [K["qqb"], K["qstep"]], [K["qub"]])
        dq3 = dq_ap.rearrange("p (n b) -> p n b", b=32)
        qub3 = d["qub"][:].rearrange("p (n b) -> p n b", b=32)
        S("dve", lambda e: e.tensor_tensor(dq3, qub3, sb_b, OP.mult),
          [K["qub"], K["b3"]], [dq_key])

    opsA, opsB, opsC = [], [], []
    for lst, fn in ((opsA, stageA), (opsB, stageB), (opsC, stageC)):
        tgt[0] = lst
        fn()
    return opsA, opsB, opsC


class QuantSched:

    def __init__(self, pr, qp, consts):
        self.pr, self.qp, self.consts = pr, qp, consts
        self.q = []

    @staticmethod
    def _zip_emit(lists):
        i = 0
        more = True
        while more:
            more = False
            for lst in lists:
                if i < len(lst):
                    lst[i]()
                    more = True
            i += 1

    def push(self, t_ap, t_key, dq_ap, dq_key, post=None):
        A, B, C = quant_stages(self.pr, self.qp, self.consts, t_ap, t_key,
                               dq_ap, dq_key)
        lists = [A]
        pB = self.q[-1] if len(self.q) >= 1 else None
        if pB is not None:
            lists.append(pB[0])
        ent0 = None
        if len(self.q) >= 2:
            ent0 = self.q.pop(0)
            lists.append(ent0[1])
        self._zip_emit(lists)
        if ent0 is not None and ent0[2] is not None:
            ent0[2]()
        self.q.append((B, C, post))

    def flush(self):
        if self.q:
            self._zip_emit([self.q[-1][0]])
        while self.q:
            _, C0, p0 = self.q.pop(0)
            self._zip_emit([C0])
            if p0 is not None:
                p0()


def emit_quant(pr, qp, consts, t_ap, t_key, dq_ap, dq_key):
    A, B, C = quant_stages(pr, qp, consts, t_ap, t_key, dq_ap, dq_key)
    for lst in (A, B, C):
        for th in lst:
            th()


def emit_ln_stats(pr, src_aps, src_keys, mv, mv_key, stats, stats_key,
                  consts, rstd, rstd_key):
    ng = len(src_aps)
    for i, (ap, k) in enumerate(zip(src_aps, src_keys)):
        pr.step("dve", lambda e, ap=ap, i=i: e.bn_stats(
            stats[:, 6 * i:6 * i + 6], ap), reads=[k], writes=[stats_key])
    pr.step("dve", lambda e, ng=ng: e.bn_aggr(mv[:], stats[:, 0:6 * ng]),
            reads=[stats_key], writes=[mv_key])
    pr.step("act", lambda e: e.activation(rstd[:], mv[:, 1:2], AF.Ln,
                                          bias=consts["eps"][:]),
            reads=[mv_key, "c_eps"], writes=[rstd_key])
    pr.step("act", lambda e: e.activation(rstd[:], rstd[:], AF.Exp,
                                          scale=-0.5),
            reads=[rstd_key], writes=[rstd_key])


def build_mk(cfg: Cfg, dbg=False, phases=(0, 1, 2, 3, 4), relax=True):
    T, D, I, BC, TT, IW = cfg.T, cfg.D, cfg.I, cfg.BC, cfg.TT, cfg.IW
    NH = cfg.NH
    DC = D // 128
    TC = T // 128
    SUB = TT // 128
    NTILE = BC * T // TT
    NW = min(512, T)
    VW = min(512, D)
    JW = min(512, D)
    HW_ = min(512, D)
    QW = min(512, D)

    nc = bass.Bass("TRN2", target_bir_lowering=False,
                   detect_race_conditions=not relax)
    x = nc.dram_tensor("x", [BC, T, D], F32, kind="ExternalInput").ap()
    wq = nc.dram_tensor("wq", [D, D], F32, kind="ExternalInput").ap()
    wk = nc.dram_tensor("wk", [D, D], F32, kind="ExternalInput").ap()
    wv = nc.dram_tensor("wv", [D, D], F32, kind="ExternalInput").ap()
    wo = nc.dram_tensor("wo", [D, D], F32, kind="ExternalInput").ap()
    gw = nc.dram_tensor("gw", [I, D], F32, kind="ExternalInput").ap()
    uw = nc.dram_tensor("uw", [I, D], F32, kind="ExternalInput").ap()
    dw = nc.dram_tensor("dw", [D, I], F32, kind="ExternalInput").ap()
    out = nc.dram_tensor("out", [BC, T, D], F32, kind="ExternalOutput").ap()

    KIND = "ExternalOutput" if dbg else "Internal"
    qtS = nc.dram_tensor("qtS", [BC, D, T], BF16, kind=KIND).ap()
    ktS = nc.dram_tensor("ktS", [BC, D, T], BF16, kind=KIND).ap()
    vS = nc.dram_tensor("vS", [BC, T, D], BF16, kind=KIND).ap()
    aoS = nc.dram_tensor("aoS", [BC, D, T], BF16, kind=KIND).ap()
    x1S = nc.dram_tensor("x1S", [BC, T, D], F32, kind=KIND).ap()
    gwTS = nc.dram_tensor("gwTS", [D, I], BF16, kind=KIND).ap()
    upTS = nc.dram_tensor("upTS", [D, I], BF16, kind=KIND).ap()
    dwTS = nc.dram_tensor("dwTS", [I, D], BF16, kind=KIND).ap()

    pr = Prog(nc)
    pr.relax = relax
    es_top = contextlib.ExitStack()
    al = Alloc(nc, es_top)

    ident = al.sb("ident", [128, 128], BF16)
    trimask = al.sb("trimask", [128, 128], F32)
    ones_c = al.sb("ones_c", [128, 1], BF16)
    ones_r = al.sb("ones_r", [1, 128], F32)
    consts = {"eps": al.sb("c_eps", [128, 1], F32),
              "lnz": al.sb("c_lnz", [128, 1], F32),
              "b126": al.sb("c_b126", [128, 1], F32),
              "c635": al.sb("c_c635", [128, 1], F32),
              "cm05": al.sb("c_cm05", [128, 1], F32)}
    pr.step("pool", lambda e: e.memset(ident[:], 0.0), [], ["c_ident"])
    pr.step("pool", lambda e: e.affine_select(
        out=ident[:], in_=ident[:], compare_op=OP.not_equal, fill=1.0,
        base=0, pattern=[[-1, 128]], channel_multiplier=1),
        ["c_ident"], ["c_ident"])
    pr.step("pool", lambda e: e.memset(trimask[:], 1.0), [], ["c_tri"])
    pr.step("pool", lambda e: e.affine_select(
        out=trimask[:], in_=trimask[:], compare_op=OP.is_ge, fill=0.0,
        base=0, pattern=[[1, 128]], channel_multiplier=-1),
        ["c_tri"], ["c_tri"])
    pr.step("pool", lambda e: e.memset(ones_c[:], 1.0), [], ["c_ones"])
    pr.step("pool", lambda e: e.memset(ones_r[:], 1.0), [], ["c_onesr"])
    pr.step("pool", lambda e: e.memset(consts["eps"][:], 1e-5), [], ["c_eps"])
    pr.step("pool", lambda e: e.memset(consts["lnz"][:], 1e-38), [], ["c_lnz"])
    pr.step("pool", lambda e: e.memset(consts["b126"][:], -126.0 * LN2),
            [], ["c_b126"])
    pr.step("pool", lambda e: e.memset(consts["c635"][:], 63.5), [], ["c_c635"])
    pr.step("pool", lambda e: e.memset(consts["cm05"][:], -0.5), [], ["c_cm05"])


    with contextlib.ExitStack() as es0:
      if 0 in phases:
        a0 = Alloc(nc, es0)
        qp0 = QuantPool(a0, "p0", QW, cfg.QSLOTS)
        CSPLIT = 4096
        CM = max(D, min(I, CSPLIT))
        wbuf = [a0.sb(f"p0_w{s}", [128, CM], F32) for s in range(2)]
        dqb = [a0.sb(f"p0_dq{s}", [128, CM], BF16) for s in range(2)]
        ptr = [a0.ps(f"p0_pt{s}", [128, 512], BF16) for s in range(2)]
        tbuf = [a0.sb(f"p0_tb{c}", [128, 512], BF16) for c in range(CM // 128)]

        sched0 = QuantSched(pr, qp0, consts)

        def p0_weight(w_ap, wT_S, R, C, wname):
            nch = R // 128
            for c in range(nch):
                sl = c % 2
                wk_ = f"p0w{sl}"
                dk = f"p0dq{sl}"
                pr.dma("sp", wbuf[sl][:, 0:C], w_ap[c * 128:(c + 1) * 128, :],
                       writes=[wk_])

                def chunk_post(c=c, sl=sl, dk=dk, wname=wname, wT_S=wT_S,
                               C=C, nch=nch):
                    c4 = c % 4
                    for hc in range(C // 128):
                        psl = hc % 2
                        pr.step("pe", lambda e, hc=hc, sl=sl, psl=psl:
                                e.transpose(
                                    ptr[psl][:, 0:128],
                                    dqb[sl][:, hc * 128:(hc + 1) * 128],
                                    ident[:]),
                                reads=[dk, "c_ident"], writes=[f"p0pt{psl}"])
                        pr.step("dve", lambda e, hc=hc, c4=c4, psl=psl:
                                e.tensor_copy(
                                    tbuf[hc][:, c4 * 128:(c4 + 1) * 128],
                                    ptr[psl][:, 0:128]),
                                reads=[f"p0pt{psl}"], writes=[f"p0tb{hc}"])
                    if c4 == 3 or c == nch - 1:
                        g0 = (c // 4) * 512
                        gwid = (c4 + 1) * 128
                        for hc in range(C // 128):
                            pr.dma("act",
                                   wT_S[hc * 128:(hc + 1) * 128, g0:g0 + gwid],
                                   tbuf[hc][:, 0:gwid],
                                   reads=[f"p0tb{hc}"], writes=[f"{wname}TS"])

                for q in range(C // QW):
                    sched0.push(wbuf[sl][:, q * QW:(q + 1) * QW], wk_,
                                dqb[sl][:, q * QW:(q + 1) * QW], dk,
                                post=chunk_post if q == C // QW - 1 else None)

        p0_weight(gw, gwTS, I, D, "gw")
        p0_weight(uw, upTS, I, D, "uw")
        if I > CSPLIT:
            for ih in range(I // CSPLIT):
                p0_weight(dw[:, ih * CSPLIT:(ih + 1) * CSPLIT],
                          dwTS[ih * CSPLIT:(ih + 1) * CSPLIT, :],
                          D, CSPLIT, f"dw{ih}")
        else:
            p0_weight(dw, dwTS, D, I, "dw")
        sched0.flush()
    pr.barrier()

    def p1_body(b):
        with contextlib.ExitStack() as es1:
          a1 = Alloc(nc, es1)
          ngs = max(1, D // 512)
          x0t = [a1.sb(f"p1_x0_b{b}_{s}", [128, D], F32) for s in range(2)]
          stats = a1.sb(f"p1_st_b{b}", [128, 6 * ngs], F32)
          mv = a1.sb(f"p1_mv_b{b}", [128, 2], F32)
          rstd = a1.sb(f"p1_rs_b{b}", [128, 1], F32)
          lnt = [a1.sb(f"p1_ln_b{b}_{s}", [128, D], BF16) for s in range(2)]
          ptr1 = [a1.ps(f"p1_pt_b{b}_{s}", [128, 512], BF16) for s in range(2)]
          lt = [a1.sb(f"p1_lt_b{b}_{dc}", [128, T], BF16) for dc in range(DC)]
          wcb = a1.sb(f"p1_wcb_b{b}", [128, D], F32)
          wcbb = [a1.sb(f"p1_wbb_b{b}_{s}", [128, D], BF16) for s in range(2)]
          pj = [a1.ps(f"p1_pj_b{b}_{s}", [128, NW], F32) for s in range(2)]
          oslab = [a1.sb(f"p1_os_b{b}_{s}", [128, max(NW, VW)], BF16)
                   for s in range(2)]
          q1 = a1.sb(f"p1_q1_b{b}", [128, NW], F32)
          wvb = a1.sb(f"p1_wvb_b{b}", [128, DC * VW], F32)
          wvbb = [a1.sb(f"p1_wvx_b{b}_{s}", [128, DC * VW], BF16)
                  for s in range(2)]

          for tc in range(TC):
              sl = tc % 2
              xk = f"p1x{sl}"
              pr.dma("sp", x0t[sl][:], x[b, tc * 128:(tc + 1) * 128, :],
                     writes=[xk])
              srcs = ([x0t[sl][:, i * 512:(i + 1) * 512] for i in range(ngs)]
                      if D >= 512 else [x0t[sl][:, 0:D]])
              emit_ln_stats(pr, srcs, [xk] * ngs, mv, "p1mv", stats, "p1st",
                            consts, rstd, "p1rs")
              lk = f"p1ln{sl}"
              pr.step("dve", lambda e, sl=sl: e.tensor_scalar(
                  lnt[sl][:], x0t[sl][:], mv[:, 0:1], rstd[:],
                  OP.subtract, OP.mult),
                  reads=[xk, "p1mv", "p1rs"], writes=[lk])
              for dc in range(DC):
                  psl = dc % 2
                  pr.step("pe", lambda e, dc=dc, sl=sl, psl=psl: e.transpose(
                      ptr1[psl][:, 0:128],
                      lnt[sl][:, dc * 128:(dc + 1) * 128], ident[:]),
                      reads=[lk, "c_ident"], writes=[f"p1pt{psl}"])
                  pr.step("dve", lambda e, dc=dc, tc=tc, psl=psl:
                          e.tensor_copy(lt[dc][:, tc * 128:(tc + 1) * 128],
                                        ptr1[psl][:, 0:128]),
                          reads=[f"p1pt{psl}"], writes=[f"p1lt{dc}"])

          for wnm, w_ap, outS, isq in (("k", wk, ktS, False),
                                       ("q", wq, qtS, True)):
              for jb in range(DC):
                  wsl = jb % 2
                  pr.dma("sp", wcb[:, 0:D].rearrange("p (c j) -> p c j", j=128),
                         w_ap[:, jb * 128:(jb + 1) * 128].rearrange(
                             "(c p) j -> p c j", p=128),
                         writes=["p1wcb"])
                  pr.step("dve", lambda e, wsl=wsl: e.tensor_copy(
                      wcbb[wsl][:], wcb[:]),
                      reads=["p1wcb"], writes=[f"p1wbb{wsl}"])
                  for ts_ in range(T // NW):
                      psl = ts_ % 2
                      pk = f"p1pj{psl}"
                      for dc in range(DC):
                          pr.step("pe", lambda e, dc=dc, ts_=ts_, wsl=wsl,
                                  psl=psl: e.matmul(
                                      pj[psl][:],
                                      wcbb[wsl][:, dc * 128:(dc + 1) * 128],
                                      lt[dc][:, ts_ * NW:(ts_ + 1) * NW],
                                      start=(dc == 0), stop=(dc == DC - 1)),
                              reads=[f"p1wbb{wsl}", f"p1lt{dc}", pk],
                              writes=[pk])
                      osl = ts_ % 2
                      ok = f"p1os{osl}"
                      if isq:
                          pr.step("act", lambda e, psl=psl, osl=osl:
                                  e.activation(oslab[osl][:, 0:NW],
                                               pj[psl][:], AF.Identity,
                                               scale=cfg.u_hi),
                                  reads=[pk], writes=[ok])
                      else:
                          pr.step("dve", lambda e, psl=psl, osl=osl:
                                  e.tensor_copy(oslab[osl][:, 0:NW],
                                                pj[psl][:]),
                                  reads=[pk], writes=[ok])
                      pr.dma("act",
                             outS[b, jb * 128:(jb + 1) * 128,
                                  ts_ * NW:(ts_ + 1) * NW],
                             oslab[osl][:, 0:NW], reads=[ok],
                             writes=[f"{wnm}S"])

          for dvb in range(max(1, D // VW)):
              wsl = dvb % 2
              pr.dma("sp", wvb[:].rearrange("p (c j) -> p c j", j=VW),
                     wv[:, dvb * VW:(dvb + 1) * VW].rearrange(
                         "(c p) j -> p c j", p=128),
                     writes=["p1wvb"])
              pr.step("dve", lambda e, wsl=wsl: e.tensor_copy(
                  wvbb[wsl][:], wvb[:]),
                  reads=["p1wvb"], writes=[f"p1wvx{wsl}"])
              for tc in range(TC):
                  psl = tc % 2
                  pk = f"p1pj{psl}"
                  for dc in range(DC):
                      pr.step("pe", lambda e, dc=dc, tc=tc, wsl=wsl, psl=psl:
                              e.matmul(
                                  pj[psl][:, 0:VW],
                                  lt[dc][:, tc * 128:(tc + 1) * 128],
                                  wvbb[wsl][:, dc * VW:(dc + 1) * VW],
                                  start=(dc == 0), stop=(dc == DC - 1)),
                              reads=[f"p1wvx{wsl}", f"p1lt{dc}", pk],
                              writes=[pk])
                  osl = tc % 2
                  ok = f"p1os{osl}"
                  pr.step("dve", lambda e, psl=psl, osl=osl: e.tensor_copy(
                      oslab[osl][:, 0:VW], pj[psl][:, 0:VW]),
                      reads=[pk], writes=[ok])
                  pr.dma("act",
                         vS[b, tc * 128:(tc + 1) * 128,
                            dvb * VW:(dvb + 1) * VW],
                         oslab[osl][:, 0:VW], reads=[ok], writes=["vS"])
    def p2_body(b):
        with contextlib.ExitStack() as es2:
          a2 = Alloc(nc, es2)
          sqt = [a2.sb(f"p2_q_b{b}_{s}", [128, TT], BF16) for s in range(2)]
          skt = [a2.sb(f"p2_k_b{b}_{s}", [128, T], BF16) for s in range(2)]
          svt = [a2.sb(f"p2_v_b{b}_{s}", [128, T], BF16) for s in range(2)]
          atb = a2.sb(f"p2_at_b{b}", [128, (T // 128) * TT], BF16)
          mskt = a2.sb(f"p2_msk_b{b}", [128, 128], F32)
          ps_s = [a2.ps(f"p2_ps_b{b}_{s}", [128, TT], F32) for s in range(2)]
          ps_cs = a2.ps(f"p2_cs_b{b}", [128, T // 128], F32)
          cs_sb = a2.sb(f"p2_cssb_b{b}", [128, T // 128], F32)
          sv_sb = a2.sb(f"p2_svsb_b{b}", [128, T // 128 + 1], F32)
          ones_cf = a2.sb(f"p2_onescf_b{b}", [128, 1], BF16)
          rdf = a2.sb(f"p2_rdf_b{b}", [1, TT], F32)
          ps_o = [a2.ps(f"p2_po_b{b}_{s}", [128, TT], F32)
                  for s in range(2)]
          ps_d = a2.ps(f"p2_pd_b{b}", [1, TT], F32)
          ps_b = a2.ps(f"p2_pb_b{b}", [128, TT], F32)
          rd = a2.sb(f"p2_rd_b{b}", [1, TT], F32)
          bcast = a2.sb(f"p2_bc_b{b}", [128, TT], F32)
          aob = [a2.sb(f"p2_ao_b{b}_{s}", [128, TT], BF16) for s in range(2)]
          for h in range(NH):
              ksl = h % 2
              kk, vk = f"p2k{ksl}", f"p2v{ksl}"
              pr.dma("sp", skt[ksl][:], ktS[b, h * 128:(h + 1) * 128, :],
                     writes=[kk])
              pr.dma("sp", svt[ksl][:].rearrange("p (c d) -> p c d", d=128),
                     vS[b, :, h * 128:(h + 1) * 128].rearrange(
                         "(c p) d -> p c d", p=128),
                     writes=[vk])
              if h == 0:
                  pr.step("pool", lambda e: e.memset(ones_cf[:], 1.0),
                          [], ["p2onescf"])
              for sc in range(T // 128):
                  pr.step("pe", lambda e, sc=sc, ksl=ksl: e.matmul(
                      ps_cs[:, sc:sc + 1],
                      svt[ksl][:, sc * 128:(sc + 1) * 128],
                      ones_cf[:], start=True, stop=True),
                      reads=[vk, "p2onescf", "p2cs"], writes=["p2cs"])
              pr.step("dve", lambda e: e.tensor_copy(cs_sb[:], ps_cs[:]),
                      reads=["p2cs"], writes=["p2cssb"])
              pr.step("pool", lambda e: e.memset(
                  sv_sb[:, T // 128:T // 128 + 1], 0.0),
                  [], ["p2svsb"])
              for c in range(T // 128 - 1, -1, -1):
                  pr.step("dve", lambda e, c=c: e.tensor_tensor(
                      sv_sb[:, c:c + 1], sv_sb[:, c + 1:c + 2],
                      cs_sb[:, c:c + 1], OP.add),
                      reads=["p2svsb", "p2cssb"], writes=["p2svsb"])
              for ts_ in range(T // TT):
                  smax = ((ts_ + 1) * TT) // 128
                  qsl = ts_ % 2
                  qk = f"p2q{qsl}"
                  pr.dma("sp", sqt[qsl][:],
                         qtS[b, h * 128:(h + 1) * 128,
                             ts_ * TT:(ts_ + 1) * TT],
                         writes=[qk])
                  for sc in range(smax):
                      psl = sc % 2
                      pk = f"p2ps{psl}"
                      pr.step("pe", lambda e, sc=sc, qsl=qsl, psl=psl,
                              ksl=ksl: e.matmul(
                                  ps_s[psl][:],
                                  skt[ksl][:, sc * 128:(sc + 1) * 128],
                                  sqt[qsl][:], start=True, stop=True),
                              reads=[kk, qk, pk], writes=[pk])
                      dpos = sc - ts_ * SUB
                      if dpos > 0:
                          w_ = min(dpos, SUB) * 128
                          pr.step("pool", lambda e, sc=sc, w_=w_: e.memset(
                              atb[:, sc * TT:sc * TT + w_], 1.0),
                              reads=[], writes=["p2at"])
                      if 0 <= dpos < SUB:
                          pr.step("dve", lambda e, dpos=dpos, psl=psl:
                                  e.tensor_tensor(
                                      mskt[:],
                                      ps_s[psl][:,
                                                dpos * 128:(dpos + 1) * 128],
                                      trimask[:], OP.mult),
                                  reads=[pk, "c_tri"], writes=["p2msk"])
                          pr.step("act", lambda e, sc=sc, dpos=dpos:
                                  e.activation(
                                      atb[:, sc * TT + dpos * 128:
                                          sc * TT + (dpos + 1) * 128],
                                      mskt[:], AF.Exp, scale=1.0),
                                  reads=["p2msk"], writes=["p2at"])
                      lo = max(dpos + 1, 0)
                      if lo < SUB:
                          pr.step("act", lambda e, sc=sc, lo=lo, psl=psl:
                                  e.activation(
                                      atb[:, sc * TT + lo * 128:
                                          sc * TT + SUB * 128],
                                      ps_s[psl][:, lo * 128:SUB * 128],
                                      AF.Exp, scale=1.0),
                                  reads=[pk], writes=["p2at"])
                  osl_ = ts_ % 2
                  pok = f"p2po{osl_}"
                  for sc in range(smax):
                      pr.step("pe", lambda e, sc=sc, ksl=ksl, smax=smax,
                              osl_=osl_: e.matmul(
                                  ps_o[osl_][:],
                                  svt[ksl][:, sc * 128:(sc + 1) * 128],
                                  atb[:, sc * TT:(sc + 1) * TT],
                                  start=(sc == 0), stop=(sc == smax - 1)),
                              reads=[vk, "p2at", pok], writes=[pok])
                  for sc in range(smax):
                      pr.step("pe", lambda e, sc=sc, smax=smax: e.matmul(
                          ps_d[:], ones_c[:],
                          atb[:, sc * TT:(sc + 1) * TT],
                          start=(sc == 0), stop=(sc == smax - 1)),
                          reads=["c_ones", "p2at", "p2pd"], writes=["p2pd"])
                  cnt = float(T - smax * 128)
                  pr.step("dve", lambda e, cnt=cnt: e.tensor_single_scalar(
                      rdf[:], ps_d[:], cnt, OP.add),
                      reads=["p2pd"], writes=["p2rdf"])
                  pr.step("dve", lambda e: e.reciprocal(rd[:], rdf[:]),
                          reads=["p2rdf"], writes=["p2rd"])
                  pr.step("pe", lambda e: e.matmul(
                      ps_b[:], ones_r[:], rd[:], start=True, stop=True),
                      reads=["c_onesr", "p2rd", "p2pb"], writes=["p2pb"])
                  pr.step("dve", lambda e: e.tensor_copy(bcast[:], ps_b[:]),
                          reads=["p2pb"], writes=["p2bc"])
                  asl = ts_ % 2
                  if smax * 128 < T:
                      pr.step("dve", lambda e, asl=asl, smax=smax,
                              osl_=osl_: e.scalar_tensor_tensor(
                                  aob[asl][:], ps_o[osl_][:],
                                  sv_sb[:, smax:smax + 1], bcast[:],
                                  OP.add, OP.mult),
                              reads=[pok, "p2svsb", "p2bc"],
                              writes=[f"p2ao{asl}"])
                  else:
                      pr.step("dve", lambda e, asl=asl, osl_=osl_:
                              e.tensor_tensor(
                                  aob[asl][:], ps_o[osl_][:], bcast[:],
                                  OP.mult),
                              reads=[pok, "p2bc"], writes=[f"p2ao{asl}"])
                  pr.dma("act",
                         aoS[b, h * 128:(h + 1) * 128,
                             ts_ * TT:(ts_ + 1) * TT],
                         aob[asl][:], reads=[f"p2ao{asl}"], writes=["aoS"])
    for b in range(BC):
        if 1 in phases:
            p1_body(b)
        pr.barrier()
        if 2 in phases:
            p2_body(b)
        pr.barrier()

    with contextlib.ExitStack() as es3:
      if 3 in phases:
        a3 = Alloc(nc, es3)
        ngs = max(1, D // 512)
        wob = a3.sb("p3_wo", [128, DC * D], BF16)
        wof = a3.sb("p3_wof", [128, D], F32)
        aosb = [a3.sb(f"p3_ao{s}", [128, DC * TT], BF16) for s in range(2)]
        x0tt = [a3.sb(f"p3_x0{s}", [128, D], F32) for s in range(2)]
        x1t = [a3.sb(f"p3_x1{s}", [128, D], F32) for s in range(2)]
        stats3 = a3.sb("p3_st", [128, 6 * ngs], F32)
        mv3 = a3.sb("p3_mv", [128, 2], F32)
        rstd3 = a3.sb("p3_rs", [128, 1], F32)
        po2 = [a3.ps(f"p3_po{j}", [128, JW], F32) for j in range(D // JW)]
        for dc in range(DC):
            pr.dma("sp", wof[:], wo[dc * 128:(dc + 1) * 128, :],
                   writes=["p3wof"])
            pr.step("dve", lambda e, dc=dc: e.tensor_copy(
                wob[:, dc * D:(dc + 1) * D], wof[:]),
                reads=["p3wof"], writes=["p3wo"])
        for tt in range(NTILE):
            b, lo = divmod(tt * TT, T)
            asl = tt % 2
            aok = f"p3ao{asl}"
            pr.dma("sp", aosb[asl][:].rearrange("p (c t) -> p c t", t=TT),
                   aoS[b, :, lo:lo + TT].rearrange("(c p) t -> p c t",
                                                   p=128),
                   writes=[aok])
            for s4 in range(SUB):
                xsl = s4 % 2
                pr.dma("sp", x0tt[xsl][:],
                       x[b, lo + s4 * 128:lo + (s4 + 1) * 128, :],
                       writes=[f"p3x0{xsl}"])
                for jb in range(D // JW):
                    pk = f"p3po{jb}"
                    for dc in range(DC):
                        pr.step("pe", lambda e, dc=dc, s4=s4, jb=jb, asl=asl:
                                e.matmul(
                                    po2[jb][:],
                                    aosb[asl][:, dc * TT + s4 * 128:
                                              dc * TT + (s4 + 1) * 128],
                                    wob[:, dc * D + jb * JW:
                                        dc * D + (jb + 1) * JW],
                                    start=(dc == 0), stop=(dc == DC - 1)),
                                reads=[aok, "p3wo", pk], writes=[pk])
                emit_ln_stats(pr, [po2[j][:] for j in range(D // JW)],
                              [f"p3po{j}" for j in range(D // JW)],
                              mv3, "p3mv", stats3, "p3st", consts,
                              rstd3, "p3rs")
                for jb in range(D // JW):
                    pr.step("dve", lambda e, jb=jb, xsl=xsl: e.tensor_scalar(
                        x1t[xsl][:, jb * JW:(jb + 1) * JW], po2[jb][:],
                        mv3[:, 0:1], rstd3[:], OP.subtract, OP.mult),
                        reads=[f"p3po{jb}", "p3mv", "p3rs"],
                        writes=[f"p3x1{xsl}"])
                pr.step("dve", lambda e, xsl=xsl: e.tensor_tensor(
                    x1t[xsl][:], x1t[xsl][:], x0tt[xsl][:], OP.add),
                    reads=[f"p3x1{xsl}", f"p3x0{xsl}"], writes=[f"p3x1{xsl}"])
                pr.dma("act", x1S[b, lo + s4 * 128:lo + (s4 + 1) * 128, :],
                       x1t[xsl][:], reads=[f"p3x1{xsl}"], writes=["x1S"])
    pr.barrier()

    ICN = I // IW
    NPART = getattr(cfg, 'NPART_OVR', 8)
    ICP = ICN // NPART
    assert ICN % NPART == 0 and (I // NPART) % QW == 0
    HB = I // NPART
    with contextlib.ExitStack() as es4:
      if 4 in phases:
        a4 = Alloc(nc, es4)
        ngs = max(1, D // 512)
        qp4 = QuantPool(a4, "p4", QW, cfg.QSLOTS)
        x1in = [a4.sb(f"p4_x1{s}", [128, D], F32) for s in range(2)]
        stats4 = a4.sb("p4_st", [128, 6 * ngs], F32)
        mv4 = a4.sb("p4_mv", [128, 2], F32)
        rstd4 = a4.sb("p4_rs", [128, 1], F32)
        hdq = [a4.sb(f"p4_hdq{s}", [128, QW], BF16) for s in range(3)]
        hdqT = [a4.sb(f"p4_hT{ic}", [128, TT], BF16) for ic in range(I // 128)]
        xdqT = [a4.sb(f"p4_xT{dc}", [128, TT], BF16) for dc in range(DC)]
        ptr4 = [a4.ps(f"p4_pt{s}", [128, 512], BF16) for s in range(2)]
        ps_g = a4.ps("p4_pg", [128, IW], F32)
        ps_u = a4.ps("p4_pu", [128, IW], F32)
        ps_dn = [a4.ps(f"p4_pd{k}", [128, HW_], F32) for k in range(SUB)]

        def p4_tile(tt):
            b, lo = divmod(tt * TT, T)
            pr.cur_tag = f'P4t{tt}'
            with contextlib.ExitStack() as esp:
                ap_ = Alloc(nc, esp)
                l2t = ap_.sb(f"p4_l2_t{tt}", [128, D], F32)
                xdq = [ap_.sb(f"p4_xdq_t{tt}_{s}", [128, D], BF16)
                       for s in range(2)]
                sched4 = QuantSched(pr, qp4, consts)
                for s4 in range(SUB):
                    xsl = s4 % 2
                    xk = f"p4x1{xsl}"
                    pr.dma("sp", x1in[xsl][:],
                           x1S[b, lo + s4 * 128:lo + (s4 + 1) * 128, :],
                           writes=[xk])
                    srcs = ([x1in[xsl][:, i * 512:(i + 1) * 512]
                             for i in range(ngs)] if D >= 512
                            else [x1in[xsl][:, 0:D]])
                    emit_ln_stats(pr, srcs, [xk] * ngs, mv4, "p4mv", stats4,
                                  "p4st", consts, rstd4, "p4rs")
                    pr.step("dve", lambda e, xsl=xsl: e.tensor_scalar(
                        l2t[:], x1in[xsl][:], mv4[:, 0:1], rstd4[:],
                        OP.subtract, OP.mult),
                        reads=[xk, "p4mv", "p4rs"], writes=["p4l2"])

                    def x_post(s4=s4, xsl=xsl):
                        xq = f"p4xdq{xsl}"
                        for dc in range(DC):
                            psl = dc % 2
                            pr.step("pe", lambda e, dc=dc, xsl=xsl, psl=psl:
                                    e.transpose(
                                        ptr4[psl][:, 0:128],
                                        xdq[xsl][:, dc * 128:(dc + 1) * 128],
                                        ident[:]),
                                    reads=[xq, "c_ident"],
                                    writes=[f"p4pt{psl}"])
                            pr.step("dve", lambda e, dc=dc, s4=s4, psl=psl:
                                    e.tensor_copy(
                                        xdqT[dc][:, s4 * 128:(s4 + 1) * 128],
                                        ptr4[psl][:, 0:128]),
                                    reads=[f"p4pt{psl}"], writes=[f"p4xT{dc}"])

                    nq = max(1, D // QW)
                    for q in range(nq):
                        sched4.push(l2t[:, q * QW:(q + 1) * QW], "p4l2",
                                    xdq[xsl][:, q * QW:(q + 1) * QW],
                                    f"p4xdq{xsl}",
                                    post=x_post if q == nq - 1 else None)
                sched4.flush()
            pr.barrier()
            with contextlib.ExitStack() as esm:
                am = Alloc(nc, esm)
                sched4m = QuantSched(pr, qp4, consts)
                hq_counter = [0]
                g_rhs = [am.sb(f"p4_gr_t{tt}_{s}", [128, DC * IW], BF16)
                         for s in range(2)]
                u_rhs = [am.sb(f"p4_ur_t{tt}_{s}", [128, DC * IW], BF16)
                         for s in range(2)]
                sg = [am.sb(f"p4_sg_t{tt}_{s}", [128, IW], BF16) for s in range(2)]
                hbuf = [[am.sb(f"p4_h_t{tt}_{p}_{k}", [128, HB], BF16)
                         for k in range(SUB)] for p in range(2)]
                for part in range(NPART):
                    pp = part % 2
                    for ici in range(ICP):
                        ic = part * ICP + ici
                        wsl = ic % 2
                        gk, uk = f"p4gr{wsl}", f"p4ur{wsl}"
                        pr.dma("sp", g_rhs[wsl][:].rearrange(
                                   "p (c i) -> p c i", i=IW),
                               gwTS[:, ic * IW:(ic + 1) * IW].rearrange(
                                   "(c p) i -> p c i", p=128),
                               writes=[gk])
                        pr.dma("sp", u_rhs[wsl][:].rearrange(
                                   "p (c i) -> p c i", i=IW),
                               upTS[:, ic * IW:(ic + 1) * IW].rearrange(
                                   "(c p) i -> p c i", p=128),
                               writes=[uk])
                        for s4 in range(SUB):
                            hk = f"p4h{pp}_{s4}"
                            gp = s4 % 2
                            pg = ps_g if gp == 0 else ps_dn[0]
                            pu = ps_u if gp == 0 else ps_dn[1]
                            pgk, puk = f"p4pg{gp}", f"p4pu{gp}"
                            sgk = f"p4sg{gp}"
                            for dc in range(DC):
                                pr.step("pe", lambda e, dc=dc, s4=s4, wsl=wsl,
                                        pg=pg: e.matmul(
                                            pg[:, 0:IW],
                                            xdqT[dc][:,
                                                     s4 * 128:(s4 + 1) * 128],
                                            g_rhs[wsl][:,
                                                       dc * IW:(dc + 1) * IW],
                                            start=(dc == 0),
                                            stop=(dc == DC - 1)),
                                        reads=[f"p4xT{dc}", gk, pgk],
                                        writes=[pgk])
                            for dc in range(DC):
                                pr.step("pe", lambda e, dc=dc, s4=s4, wsl=wsl,
                                        pu=pu: e.matmul(
                                            pu[:, 0:IW],
                                            xdqT[dc][:,
                                                     s4 * 128:(s4 + 1) * 128],
                                            u_rhs[wsl][:,
                                                       dc * IW:(dc + 1) * IW],
                                            start=(dc == 0),
                                            stop=(dc == DC - 1)),
                                        reads=[f"p4xT{dc}", uk, puk],
                                        writes=[puk])
                            pr.step("act", lambda e, pg=pg, gp=gp:
                                    e.activation(sg[gp][:], pg[:, 0:IW],
                                                 AF.Silu),
                                    reads=[pgk], writes=[sgk])
                            pr.step("dve", lambda e, pp=pp, s4=s4, ici=ici,
                                    pu=pu, gp=gp: e.tensor_tensor(
                                        hbuf[pp][s4][:,
                                                     ici * IW:(ici + 1) * IW],
                                        pu[:, 0:IW], sg[gp][:], OP.mult),
                                    reads=[puk, sgk], writes=[hk])
                    for s4 in range(SUB):
                        hk = f"p4h{pp}_{s4}"
                        for q5 in range(HB // QW):
                            hsl = hq_counter[0] % 3
                            hq_counter[0] += 1
                            hqk = f"p4hdq{hsl}"

                            def h_post(s4=s4, q5=q5, hsl=hsl, hqk=hqk,
                                       part=part):
                                for j4 in range(QW // 128):
                                    psl = j4 % 2
                                    icc = (part * HB // 128
                                           + q5 * (QW // 128) + j4)
                                    pr.step("pe", lambda e, j4=j4, hsl=hsl,
                                            psl=psl: e.transpose(
                                                ptr4[psl][:, 0:128],
                                                hdq[hsl][:, j4 * 128:
                                                          (j4 + 1) * 128],
                                                ident[:]),
                                            reads=[hqk, "c_ident"],
                                            writes=[f"p4pt{psl}"])
                                    pr.step("dve", lambda e, icc=icc, s4=s4,
                                            psl=psl: e.tensor_copy(
                                                hdqT[icc][:, s4 * 128:
                                                          (s4 + 1) * 128],
                                                ptr4[psl][:, 0:128]),
                                            reads=[f"p4pt{psl}"],
                                            writes=[f"p4hT{icc}"])

                            sched4m.push(
                                hbuf[pp][s4][:, q5 * QW:(q5 + 1) * QW],
                                hk, hdq[hsl][:], hqk, post=h_post)
                sched4m.flush()
            pr.barrier()
            with contextlib.ExitStack() as esc:
                ac = Alloc(nc, esc)
                d_rhs = [ac.sb(f"p4_dr_t{tt}_{s}", [128, HW_], BF16)
                         for s in range(2)]
                fo = [ac.sb(f"p4_fo_t{tt}_{k}", [128, D], F32) for k in range(SUB)]
                outt = ac.sb(f"p4_out_t{tt}", [128, D], F32)
                for hb in range(D // HW_):
                    for icc in range(I // 128):
                        dsl = icc % 2
                        dk_ = f"p4dr{dsl}"
                        pr.dma("sp", d_rhs[dsl][:],
                               dwTS[icc * 128:(icc + 1) * 128,
                                    hb * HW_:(hb + 1) * HW_],
                               writes=[dk_])
                        for s4 in range(SUB):
                            pr.step("pe", lambda e, icc=icc, s4=s4, dsl=dsl:
                                    e.matmul(
                                        ps_dn[s4][:],
                                        hdqT[icc][:, s4 * 128:(s4 + 1) * 128],
                                        d_rhs[dsl][:],
                                        start=(icc == 0),
                                        stop=(icc == I // 128 - 1)),
                                    reads=[f"p4hT{icc}", dk_, f"p4pd{s4}"],
                                    writes=[f"p4pd{s4}"])
                    for s4 in range(SUB):
                        pr.step("dve", lambda e, s4=s4, hb=hb: e.tensor_copy(
                            fo[s4][:, hb * HW_:(hb + 1) * HW_], ps_dn[s4][:]),
                            reads=[f"p4pd{s4}"], writes=[f"p4fo{s4}"])
                for s4 in range(SUB):
                    xsl = s4 % 2
                    xk = f"p4x1{xsl}"
                    pr.dma("sp", x1in[xsl][:],
                           x1S[b, lo + s4 * 128:lo + (s4 + 1) * 128, :],
                           writes=[xk])
                    srcs = ([fo[s4][:, i * 512:(i + 1) * 512]
                             for i in range(ngs)] if D >= 512
                            else [fo[s4][:, 0:D]])
                    emit_ln_stats(pr, srcs, [f"p4fo{s4}"] * ngs, mv4, "p4mv",
                                  stats4, "p4st", consts, rstd4, "p4rs")
                    pr.step("dve", lambda e, s4=s4: e.tensor_scalar(
                        outt[:], fo[s4][:], mv4[:, 0:1], rstd4[:],
                        OP.subtract, OP.mult),
                        reads=[f"p4fo{s4}", "p4mv", "p4rs"], writes=["p4out"])
                    pr.step("dve", lambda e, xsl=xsl: e.tensor_tensor(
                        outt[:], outt[:], x1in[xsl][:], OP.add),
                        reads=["p4out", xk], writes=["p4out"])
                    pr.dma("act",
                           out[b, lo + s4 * 128:lo + (s4 + 1) * 128, :],
                           outt[:], reads=["p4out"], writes=["outD"])
            pr.barrier()

        for tt in range(NTILE):
            p4_tile(tt)
    es_top.close()

    with contextlib.ExitStack() as esf:
        sems = {e: esf.enter_context(nc.semaphore(f"sem_{e}"))
                for e in ENGINES}
        for q in ("sp", "act", "pool"):
            for ln in range(Prog.NLANES):
                sems[(q, ln)] = esf.enter_context(
                    nc.semaphore(f"sem_{q}{ln}"))
        blk = esf.enter_context(nc.Block())
        pr.emit(blk, sems)
    return nc


from concourse.bass_utils import run_bass_kernel_spmd

_NC_CACHE = {}
RELAX = False


def kernel(L0_attn_x, L0_attn_Wq, L0_attn_Wk, L0_attn_Wv, L0_attn_Wo,
           L0_ffn_gate_w, L0_ffn_up_w, L0_ffn_down_w):
    x = np.asarray(L0_attn_x)
    B, T, D = x.shape
    n_cores = 8
    BC = B // n_cores
    cfg = Cfg(T=T, D=D, I=np.asarray(L0_ffn_gate_w).shape[0], BC=BC)
    key = (T, D, cfg.I, BC)
    if key not in _NC_CACHE:
        _NC_CACHE[key] = build_mk(cfg, relax=RELAX)
    nc = _NC_CACHE[key]
    w = {
        "wq": np.ascontiguousarray(L0_attn_Wq),
        "wk": np.ascontiguousarray(L0_attn_Wk),
        "wv": np.ascontiguousarray(L0_attn_Wv),
        "wo": np.ascontiguousarray(L0_attn_Wo),
        "gw": np.ascontiguousarray(L0_ffn_gate_w),
        "uw": np.ascontiguousarray(L0_ffn_up_w),
        "dw": np.ascontiguousarray(L0_ffn_down_w),
    }
    in_maps = [dict(w, x=np.ascontiguousarray(x[c * BC:(c + 1) * BC]))
               for c in range(n_cores)]
    res = run_bass_kernel_spmd(nc, in_maps, core_ids=list(range(n_cores)))
    out = np.concatenate([np.asarray(r["out"]).reshape(BC, T, D)
                          for r in res.results], axis=0)
    return out.astype(np.float32)


# version 15
# speedup vs baseline: 1.1434x; 1.1434x over previous
import math
import contextlib
import numpy as np
import concourse.bass as bass
import concourse.mybir as mybir

F32 = mybir.dt.float32
BF16 = mybir.dt.bfloat16
I32 = mybir.dt.int32
AF = mybir.ActivationFunctionType
OP = mybir.AluOpType
AX = mybir.AxisListType

ENGINES = ("pe", "act", "dve", "pool", "sp")

LN2 = float(np.log(2.0))
C_LG = float(1.0 / (2.0 * np.log(2.0)))
C_HI = float(np.float32(LN2))
C_LO = float(np.float32(np.float64(LN2) - np.float64(np.float32(LN2))))
QUANT_COMPARE = True
DEBUG_ANNOTATE = False
C1_448 = float(np.float32(1.0 / 448.0))
C3_448 = float(np.float32(1.0 / 448.0 - np.float64(np.float32(1.0 / 448.0))))


class Cfg:
    def __init__(self, T=2048, D=2048, I=8192, BC=2, TT=512, IW=256,
                 QSLOTS=3):
        self.T, self.D, self.I, self.BC, self.TT, self.IW = T, D, I, BC, TT, IW
        self.NH = D // 128
        self.QSLOTS = QSLOTS
        assert D % 128 == 0 and T % TT == 0 and TT % 128 == 0
        assert I % IW == 0 and I % 512 == 0
        u = 1.0 / math.sqrt(T)
        self.u_hi = float(np.float32(u))
        self.u_lo = float(np.float32(np.float64(u) - np.float64(self.u_hi)))


class Prog:
    NLANES = 8

    def __init__(self, nc):
        self.nc = nc
        self.cur_tag = ""
        self.steps = []
        self.last_w = {}
        self.readers = {}
        self.bar_deps = []
        self.dma_count = {e: 0 for e in ENGINES}

    def step(self, engine, fn, reads=(), writes=(), inc=1, lane=None):
        idx = len(self.steps)
        deps = set(self.bar_deps)
        for bk in reads:
            if bk in self.last_w:
                deps.add(self.last_w[bk])
        for bk in writes:
            if bk in self.last_w:
                deps.add(self.last_w[bk])
            for r in self.readers.get(bk, {}).values():
                deps.add(r)
        deps.discard(idx)
        self.steps.append(dict(i=idx, engine=engine, fn=fn, inc=inc,
                               deps=deps, need=False, lane=lane,
                               tag=self.cur_tag, reads=list(reads),
                               writes=list(writes)))
        for bk in reads:
            self.readers.setdefault(bk, {})[engine] = idx
        for bk in writes:
            self.last_w[bk] = idx
            self.readers[bk] = {}
        return idx

    def dma(self, queue, out, in_, reads=(), writes=()):
        lane = self.dma_count[queue] % self.NLANES
        self.dma_count[queue] += 1
        return self.step(
            queue, lambda e, o=out, i=in_: e.dma_start(out=o, in_=i),
            reads=reads, writes=writes, inc=16, lane=lane)

    def barrier(self):
        last = {}
        for st in self.steps:
            last[(st["engine"], st["lane"])] = st["i"]
        self.bar_deps = list(last.values())

    relax = True

    def emit(self, block, sems):
        for st in self.steps:
            if st["lane"] is not None:
                st["need"] = True
        SELF_ORDERED = ("pe", "dve", "act") if self.relax else ("pe",)

        def implicit(ds, st):
            return (ds["engine"] == st["engine"] and st["lane"] is None
                    and ds["lane"] is None and ds["engine"] in SELF_ORDERED)

        for st in self.steps:
            for d in st["deps"]:
                ds = self.steps[d]
                if implicit(ds, st):
                    continue
                ds["need"] = True
        cum = {e: 0 for e in ENGINES}
        lcum = {}
        prev_on_lane = {}
        for st in self.steps:
            if st["lane"] is not None:
                key = (st["engine"], st["lane"])
                lcum[key] = lcum.get(key, 0) + 16
                st["cum"] = lcum[key]
                st["semkey"] = key
                st["prev_lane_cum"] = st["cum"] - 16
            else:
                if st["need"]:
                    cum[st["engine"]] += st["inc"]
                st["cum"] = cum[st["engine"]]
                st["semkey"] = st["engine"]
        for st in self.steps:
            w = {}
            for d in st["deps"]:
                ds = self.steps[d]
                if implicit(ds, st):
                    continue
                w[ds["semkey"]] = max(w.get(ds["semkey"], 0), ds["cum"])
            st["waits"] = w
        finals = dict(cum)
        finals.update(lcum)
        nst = {e: sum(1 for s in self.steps if s["engine"] == e)
               for e in ENGINES}
        print("prog steps:", {e: n for e, n in nst.items() if n},
              "sem finals:", {k: v for k, v in finals.items() if v},
              flush=True)

        def mk_runner(name):
            def run(eng):
                waited = {}
                for st in self.steps:
                    if st["engine"] != name:
                        continue
                    if st["lane"] is not None and st["prev_lane_cum"] > 0:
                        sk = st["semkey"]
                        if waited.get(sk, -1) < st["prev_lane_cum"]:
                            eng.wait_ge(sems[sk], st["prev_lane_cum"])
                            waited[sk] = st["prev_lane_cum"]
                    for sk, v in sorted(st["waits"].items(), key=str):
                        if v > 0 and waited.get(sk, -1) < v:
                            eng.wait_ge(sems[sk], v)
                            waited[sk] = v
                    inst = st["fn"](eng)
                    if DEBUG_ANNOTATE:
                        inst.annotate(f"{st['tag']}#i{st['i']}")
                    if st["need"]:
                        inst.then_inc(sems[st["semkey"]], st["inc"])
                if name == "sp":
                    for k, v in sorted(finals.items(), key=str):
                        if isinstance(k, tuple) and v > 0 and waited.get(k, -1) < v:
                            eng.wait_ge(sems[k], v)
            return run

        block.tensor(mk_runner("pe"))
        block.scalar(mk_runner("act"))
        block.vector(mk_runner("dve"))
        block.gpsimd(mk_runner("pool"))
        block.sync(mk_runner("sp"))


class Alloc:
    def __init__(self, nc, es):
        self.nc, self.es = nc, es

    def sb(self, name, shape, dtype):
        return self.es.enter_context(self.nc.sbuf_tensor(name, shape, dtype))

    def ps(self, name, shape, dtype):
        return self.es.enter_context(self.nc.psum_tensor(name, shape, dtype))


class QuantPool:
    def __init__(self, al, tag, F, slots):
        self.F = F
        self.slots = slots
        self.t = []
        self.tag = tag
        for s in range(slots):
            d = {}
            for nm in ("qy", "qsq", "ql", "qtau", "qm"):
                d[nm] = al.sb(f"{tag}_{nm}{s}", [128, F], F32)
            for nm in ("qfe", "qf"):
                d[nm] = al.sb(f"{tag}_{nm}{s}", [128, F], I32)
            for nm in ("qstep", "qrstep", "qub", "qqb"):
                d[nm] = al.sb(f"{tag}_{nm}{s}", [128, F], BF16)
            for nm in ("b0", "b1", "b2"):
                d[nm] = al.sb(f"{tag}_{nm}{s}", [128, F // 32], F32)
            d["b3"] = al.sb(f"{tag}_b3{s}", [128, F // 32], BF16)
            self.t.append(d)
        self.i = 0

    def next(self):
        s = self.i % self.slots
        self.i += 1
        d = self.t[s]
        keys = {nm: f"{self.tag}{s}_{nm}" for nm in d}
        return d, keys


def quant_stages(pr, qp, consts, t_ap, t_key, dq_ap, dq_key):
    F = qp.F
    d, K = qp.next()
    nb = F // 32
    t3 = t_ap.rearrange("p (n b) -> p n b", b=32)
    lnz, b126 = consts["lnz"], consts["b126"]
    c635, cm05 = consts["c635"], consts["cm05"]
    tgt = [None]

    def S(eng, fn, reads, writes):
        tgt[0].append(lambda eng=eng, fn=fn, reads=reads, writes=writes:
                      pr.step(eng, fn, reads=reads, writes=writes))

    bm, s0, s2, s3 = d["b0"], d["b1"], d["b2"], d["b3"]

    def stageA():
        S("dve", lambda e: e.tensor_reduce(bm[:].unsqueeze(2), t3, AX.X,
                                           OP.max, apply_absolute_value=True),
          [t_key], [K["b0"]])
        S("dve", lambda e: e.tensor_scalar(s0[:], bm[:], C1_448, 1e-30,
                                           OP.mult, OP.max),
          [K["b0"]], [K["b1"]])
        S("dve", lambda e: e.reciprocal(s2[:], s0[:]), [K["b1"]], [K["b2"]])
        S("dve", lambda e: e.tensor_copy(s3[:], s0[:]), [K["b1"]], [K["b3"]])
        rs_b = s2[:].unsqueeze(2).broadcast_to([128, nb, 32])
        qy3 = d["qy"][:].rearrange("p (n b) -> p n b", b=32)
        S("dve", lambda e: e.tensor_tensor(qy3, t3, rs_b, OP.mult),
          [t_key, K["b2"]], [K["qy"]])
        S("act", lambda e: e.activation(d["qsq"][:], d["qy"][:], AF.Square),
          [K["qy"]], [K["qsq"]])
        S("act", lambda e: e.activation(d["ql"][:], d["qsq"][:], AF.Ln,
                                        bias=lnz[:]),
          [K["qsq"], "c_lnz"], [K["ql"]])

    def stageB():
        S("act", lambda e: e.activation(d["qfe"][:], d["ql"][:], AF.Identity,
                                        bias=c635[:], scale=C_LG),
          [K["ql"], "c_c635"], [K["qfe"]])
        S("act", lambda e: e.activation(d["qtau"][:], d["qfe"][:], AF.Exp,
                                        bias=b126[:], scale=2.0 * LN2),
          [K["qfe"], "c_b126"], [K["qtau"]])
        S("dve", lambda e: e.tensor_tensor(d["ql"][:], d["qsq"][:],
                                           d["qtau"][:], OP.is_ge),
          [K["qsq"], K["qtau"]], [K["ql"]])
        S("dve", lambda e: e.scalar_tensor_tensor(d["qtau"][:], d["qfe"][:],
                                                  -67.0, d["ql"][:],
                                                  OP.add, OP.add),
          [K["qfe"], K["ql"]], [K["qtau"]])
        S("dve", lambda e: e.tensor_scalar(d["qsq"][:], d["qtau"][:], -9.0,
                                           C_HI, OP.max, OP.mult),
          [K["qtau"]], [K["qsq"]])
        S("act", lambda e: e.activation(d["qstep"][:], d["qsq"][:], AF.Exp,
                                        scale=1.0),
          [K["qsq"]], [K["qstep"]])
        S("act", lambda e: e.activation(d["qrstep"][:], d["qsq"][:], AF.Exp,
                                        scale=-1.0),
          [K["qsq"]], [K["qrstep"]])

    def stageC():
        sb_b = s3[:].unsqueeze(2).broadcast_to([128, nb, 32])
        S("dve", lambda e: e.tensor_tensor(d["qm"][:], d["qy"][:],
                                           d["qrstep"][:], OP.mult),
          [K["qy"], K["qrstep"]], [K["qm"]])
        if QUANT_COMPARE:
            S("act", lambda e: e.activation(d["qf"][:], d["qm"][:],
                                            AF.Identity, bias=cm05[:]),
              [K["qm"], "c_cm05"], [K["qf"]])
            S("dve", lambda e: e.scalar_tensor_tensor(d["qub"][:], d["qm"][:],
                                                      -0.5, d["qf"][:],
                                                      OP.add, OP.is_gt),
              [K["qm"], K["qf"]], [K["qub"]])
            S("dve", lambda e: e.tensor_tensor(d["qqb"][:], d["qf"][:],
                                               d["qub"][:], OP.add),
              [K["qf"], K["qub"]], [K["qqb"]])
            S("dve", lambda e: e.tensor_tensor(d["qub"][:], d["qqb"][:],
                                               d["qstep"][:], OP.mult),
              [K["qqb"], K["qstep"]], [K["qub"]])
        else:
            S("act", lambda e: e.activation(d["qf"][:], d["qm"][:],
                                            AF.Identity),
              [K["qm"]], [K["qf"]])
            S("dve", lambda e: e.tensor_tensor(d["qub"][:], d["qf"][:],
                                                d["qstep"][:], OP.mult),
              [K["qf"], K["qstep"]], [K["qub"]])
        dq3 = dq_ap.rearrange("p (n b) -> p n b", b=32)
        qub3 = d["qub"][:].rearrange("p (n b) -> p n b", b=32)
        S("dve", lambda e: e.tensor_tensor(dq3, qub3, sb_b, OP.mult),
          [K["qub"], K["b3"]], [dq_key])

    opsA, opsB, opsC = [], [], []
    for lst, fn in ((opsA, stageA), (opsB, stageB), (opsC, stageC)):
        tgt[0] = lst
        fn()
    return opsA, opsB, opsC


class QuantSched:

    def __init__(self, pr, qp, consts):
        self.pr, self.qp, self.consts = pr, qp, consts
        self.q = []

    @staticmethod
    def _zip_emit(lists):
        i = 0
        more = True
        while more:
            more = False
            for lst in lists:
                if i < len(lst):
                    lst[i]()
                    more = True
            i += 1

    def push(self, t_ap, t_key, dq_ap, dq_key, post=None):
        A, B, C = quant_stages(self.pr, self.qp, self.consts, t_ap, t_key,
                               dq_ap, dq_key)
        lists = [A]
        pB = self.q[-1] if len(self.q) >= 1 else None
        if pB is not None:
            lists.append(pB[0])
        ent0 = None
        if len(self.q) >= 2:
            ent0 = self.q.pop(0)
            lists.append(ent0[1])
        self._zip_emit(lists)
        if ent0 is not None and ent0[2] is not None:
            ent0[2]()
        self.q.append((B, C, post))

    def flush(self):
        if self.q:
            self._zip_emit([self.q[-1][0]])
        while self.q:
            _, C0, p0 = self.q.pop(0)
            self._zip_emit([C0])
            if p0 is not None:
                p0()


def emit_quant(pr, qp, consts, t_ap, t_key, dq_ap, dq_key):
    A, B, C = quant_stages(pr, qp, consts, t_ap, t_key, dq_ap, dq_key)
    for lst in (A, B, C):
        for th in lst:
            th()


def emit_ln_stats(pr, src_aps, src_keys, mv, mv_key, stats, stats_key,
                  consts, rstd, rstd_key):
    ng = len(src_aps)
    for i, (ap, k) in enumerate(zip(src_aps, src_keys)):
        pr.step("dve", lambda e, ap=ap, i=i: e.bn_stats(
            stats[:, 6 * i:6 * i + 6], ap), reads=[k], writes=[stats_key])
    pr.step("dve", lambda e, ng=ng: e.bn_aggr(mv[:], stats[:, 0:6 * ng]),
            reads=[stats_key], writes=[mv_key])
    pr.step("act", lambda e: e.activation(rstd[:], mv[:, 1:2], AF.Ln,
                                          bias=consts["eps"][:]),
            reads=[mv_key, "c_eps"], writes=[rstd_key])
    pr.step("act", lambda e: e.activation(rstd[:], rstd[:], AF.Exp,
                                          scale=-0.5),
            reads=[rstd_key], writes=[rstd_key])


def build_mk(cfg: Cfg, dbg=False, phases=(0, 1, 2, 3, 4), relax=True):
    T, D, I, BC, TT, IW = cfg.T, cfg.D, cfg.I, cfg.BC, cfg.TT, cfg.IW
    NH = cfg.NH
    DC = D // 128
    TC = T // 128
    SUB = TT // 128
    NTILE = BC * T // TT
    NW = min(512, T)
    VW = min(512, D)
    JW = min(512, D)
    HW_ = min(512, D)
    QW = min(512, D)

    nc = bass.Bass("TRN2", target_bir_lowering=False,
                   detect_race_conditions=not relax)
    x = nc.dram_tensor("x", [BC, T, D], F32, kind="ExternalInput").ap()
    wq = nc.dram_tensor("wq", [D, D], F32, kind="ExternalInput").ap()
    wk = nc.dram_tensor("wk", [D, D], F32, kind="ExternalInput").ap()
    wv = nc.dram_tensor("wv", [D, D], F32, kind="ExternalInput").ap()
    wo = nc.dram_tensor("wo", [D, D], F32, kind="ExternalInput").ap()
    gw = nc.dram_tensor("gw", [I, D], F32, kind="ExternalInput").ap()
    uw = nc.dram_tensor("uw", [I, D], F32, kind="ExternalInput").ap()
    dw = nc.dram_tensor("dw", [D, I], F32, kind="ExternalInput").ap()
    out = nc.dram_tensor("out", [BC, T, D], F32, kind="ExternalOutput").ap()

    KIND = "ExternalOutput" if dbg else "Internal"
    qtS = nc.dram_tensor("qtS", [BC, D, T], BF16, kind=KIND).ap()
    ktS = nc.dram_tensor("ktS", [BC, D, T], BF16, kind=KIND).ap()
    vS = nc.dram_tensor("vS", [BC, T, D], BF16, kind=KIND).ap()
    aoS = nc.dram_tensor("aoS", [BC, D, T], BF16, kind=KIND).ap()
    x1S = nc.dram_tensor("x1S", [BC, T, D], F32, kind=KIND).ap()
    gwTS = nc.dram_tensor("gwTS", [D, I], BF16, kind=KIND).ap()
    upTS = nc.dram_tensor("upTS", [D, I], BF16, kind=KIND).ap()
    dwTS = nc.dram_tensor("dwTS", [I, D], BF16, kind=KIND).ap()

    pr = Prog(nc)
    pr.relax = relax
    es_top = contextlib.ExitStack()
    al = Alloc(nc, es_top)

    ident = al.sb("ident", [128, 128], BF16)
    trimask = al.sb("trimask", [128, 128], F32)
    ones_c = al.sb("ones_c", [128, 1], BF16)
    ones_r = al.sb("ones_r", [1, 128], F32)
    consts = {"eps": al.sb("c_eps", [128, 1], F32),
              "lnz": al.sb("c_lnz", [128, 1], F32),
              "b126": al.sb("c_b126", [128, 1], F32),
              "c635": al.sb("c_c635", [128, 1], F32),
              "cm05": al.sb("c_cm05", [128, 1], F32)}
    pr.step("pool", lambda e: e.memset(ident[:], 0.0), [], ["c_ident"])
    pr.step("pool", lambda e: e.affine_select(
        out=ident[:], in_=ident[:], compare_op=OP.not_equal, fill=1.0,
        base=0, pattern=[[-1, 128]], channel_multiplier=1),
        ["c_ident"], ["c_ident"])
    pr.step("pool", lambda e: e.memset(trimask[:], 1.0), [], ["c_tri"])
    pr.step("pool", lambda e: e.affine_select(
        out=trimask[:], in_=trimask[:], compare_op=OP.is_ge, fill=0.0,
        base=0, pattern=[[1, 128]], channel_multiplier=-1),
        ["c_tri"], ["c_tri"])
    pr.step("pool", lambda e: e.memset(ones_c[:], 1.0), [], ["c_ones"])
    pr.step("pool", lambda e: e.memset(ones_r[:], 1.0), [], ["c_onesr"])
    pr.step("pool", lambda e: e.memset(consts["eps"][:], 1e-5), [], ["c_eps"])
    pr.step("pool", lambda e: e.memset(consts["lnz"][:], 1e-38), [], ["c_lnz"])
    pr.step("pool", lambda e: e.memset(consts["b126"][:], -126.0 * LN2),
            [], ["c_b126"])
    pr.step("pool", lambda e: e.memset(consts["c635"][:], 63.5), [], ["c_c635"])
    pr.step("pool", lambda e: e.memset(consts["cm05"][:], -0.5), [], ["c_cm05"])


    with contextlib.ExitStack() as es0:
      if 0 in phases:
        a0 = Alloc(nc, es0)
        qp0 = QuantPool(a0, "p0", QW, cfg.QSLOTS)
        CSPLIT = 4096
        CM = max(D, min(I, CSPLIT))
        wbuf = [a0.sb(f"p0_w{s}", [128, CM], F32) for s in range(2)]
        dqb = [a0.sb(f"p0_dq{s}", [128, CM], BF16) for s in range(2)]
        ptr = [a0.ps(f"p0_pt{s}", [128, 512], BF16) for s in range(2)]
        tbuf = [a0.sb(f"p0_tb{c}", [128, 512], BF16) for c in range(CM // 128)]

        sched0 = QuantSched(pr, qp0, consts)

        def p0_weight(w_ap, wT_S, R, C, wname):
            nch = R // 128
            for c in range(nch):
                sl = c % 2
                wk_ = f"p0w{sl}"
                dk = f"p0dq{sl}"
                pr.dma("sp", wbuf[sl][:, 0:C], w_ap[c * 128:(c + 1) * 128, :],
                       writes=[wk_])

                def chunk_post(c=c, sl=sl, dk=dk, wname=wname, wT_S=wT_S,
                               C=C, nch=nch):
                    c4 = c % 4
                    for hc in range(C // 128):
                        psl = hc % 2
                        pr.step("pe", lambda e, hc=hc, sl=sl, psl=psl:
                                e.transpose(
                                    ptr[psl][:, 0:128],
                                    dqb[sl][:, hc * 128:(hc + 1) * 128],
                                    ident[:]),
                                reads=[dk, "c_ident"], writes=[f"p0pt{psl}"])
                        pr.step("dve", lambda e, hc=hc, c4=c4, psl=psl:
                                e.tensor_copy(
                                    tbuf[hc][:, c4 * 128:(c4 + 1) * 128],
                                    ptr[psl][:, 0:128]),
                                reads=[f"p0pt{psl}"], writes=[f"p0tb{hc}"])
                    if c4 == 3 or c == nch - 1:
                        g0 = (c // 4) * 512
                        gwid = (c4 + 1) * 128
                        for hc in range(C // 128):
                            pr.dma("act",
                                   wT_S[hc * 128:(hc + 1) * 128, g0:g0 + gwid],
                                   tbuf[hc][:, 0:gwid],
                                   reads=[f"p0tb{hc}"], writes=[f"{wname}TS"])

                for q in range(C // QW):
                    sched0.push(wbuf[sl][:, q * QW:(q + 1) * QW], wk_,
                                dqb[sl][:, q * QW:(q + 1) * QW], dk,
                                post=chunk_post if q == C // QW - 1 else None)

        p0_weight(gw, gwTS, I, D, "gw")
        p0_weight(uw, upTS, I, D, "uw")
        if I > CSPLIT:
            for ih in range(I // CSPLIT):
                p0_weight(dw[:, ih * CSPLIT:(ih + 1) * CSPLIT],
                          dwTS[ih * CSPLIT:(ih + 1) * CSPLIT, :],
                          D, CSPLIT, f"dw{ih}")
        else:
            p0_weight(dw, dwTS, D, I, "dw")
        sched0.flush()
    pr.barrier()

    def p1_body(b):
        with contextlib.ExitStack() as es1:
          a1 = Alloc(nc, es1)
          ngs = max(1, D // 512)
          x0t = [a1.sb(f"p1_x0_b{b}_{s}", [128, D], F32) for s in range(2)]
          stats = a1.sb(f"p1_st_b{b}", [128, 6 * ngs], F32)
          mv = a1.sb(f"p1_mv_b{b}", [128, 2], F32)
          rstd = a1.sb(f"p1_rs_b{b}", [128, 1], F32)
          lnt = [a1.sb(f"p1_ln_b{b}_{s}", [128, D], BF16) for s in range(2)]
          ptr1 = [a1.ps(f"p1_pt_b{b}_{s}", [128, 512], BF16) for s in range(2)]
          lt = [a1.sb(f"p1_lt_b{b}_{dc}", [128, T], BF16) for dc in range(DC)]
          wcb = a1.sb(f"p1_wcb_b{b}", [128, D], F32)
          wcbb = [a1.sb(f"p1_wbb_b{b}_{s}", [128, D], BF16) for s in range(2)]
          pj = [a1.ps(f"p1_pj_b{b}_{s}", [128, NW], F32) for s in range(2)]
          oslab = [a1.sb(f"p1_os_b{b}_{s}", [128, max(NW, VW)], BF16)
                   for s in range(2)]
          q1 = a1.sb(f"p1_q1_b{b}", [128, NW], F32)
          wvb = a1.sb(f"p1_wvb_b{b}", [128, DC * VW], F32)
          wvbb = [a1.sb(f"p1_wvx_b{b}_{s}", [128, DC * VW], BF16)
                  for s in range(2)]

          for tc in range(TC):
              sl = tc % 2
              xk = f"p1x{sl}"
              pr.dma("sp", x0t[sl][:], x[b, tc * 128:(tc + 1) * 128, :],
                     writes=[xk])
              srcs = ([x0t[sl][:, i * 512:(i + 1) * 512] for i in range(ngs)]
                      if D >= 512 else [x0t[sl][:, 0:D]])
              emit_ln_stats(pr, srcs, [xk] * ngs, mv, "p1mv", stats, "p1st",
                            consts, rstd, "p1rs")
              lk = f"p1ln{sl}"
              pr.step("dve", lambda e, sl=sl: e.tensor_scalar(
                  lnt[sl][:], x0t[sl][:], mv[:, 0:1], rstd[:],
                  OP.subtract, OP.mult),
                  reads=[xk, "p1mv", "p1rs"], writes=[lk])
              for dc in range(DC):
                  psl = dc % 2
                  pr.step("pe", lambda e, dc=dc, sl=sl, psl=psl: e.transpose(
                      ptr1[psl][:, 0:128],
                      lnt[sl][:, dc * 128:(dc + 1) * 128], ident[:]),
                      reads=[lk, "c_ident"], writes=[f"p1pt{psl}"])
                  pr.step("dve", lambda e, dc=dc, tc=tc, psl=psl:
                          e.tensor_copy(lt[dc][:, tc * 128:(tc + 1) * 128],
                                        ptr1[psl][:, 0:128]),
                          reads=[f"p1pt{psl}"], writes=[f"p1lt{dc}"])

          for wnm, w_ap, outS, isq in (("k", wk, ktS, False),
                                       ("q", wq, qtS, True)):
              for jb in range(DC):
                  wsl = jb % 2
                  pr.dma("sp", wcb[:, 0:D].rearrange("p (c j) -> p c j", j=128),
                         w_ap[:, jb * 128:(jb + 1) * 128].rearrange(
                             "(c p) j -> p c j", p=128),
                         writes=["p1wcb"])
                  pr.step("dve", lambda e, wsl=wsl: e.tensor_copy(
                      wcbb[wsl][:], wcb[:]),
                      reads=["p1wcb"], writes=[f"p1wbb{wsl}"])
                  for ts_ in range(T // NW):
                      psl = ts_ % 2
                      pk = f"p1pj{psl}"
                      for dc in range(DC):
                          pr.step("pe", lambda e, dc=dc, ts_=ts_, wsl=wsl,
                                  psl=psl: e.matmul(
                                      pj[psl][:],
                                      wcbb[wsl][:, dc * 128:(dc + 1) * 128],
                                      lt[dc][:, ts_ * NW:(ts_ + 1) * NW],
                                      start=(dc == 0), stop=(dc == DC - 1)),
                              reads=[f"p1wbb{wsl}", f"p1lt{dc}", pk],
                              writes=[pk])
                      osl = ts_ % 2
                      ok = f"p1os{osl}"
                      if isq:
                          pr.step("act", lambda e, psl=psl, osl=osl:
                                  e.activation(oslab[osl][:, 0:NW],
                                               pj[psl][:], AF.Identity,
                                               scale=cfg.u_hi),
                                  reads=[pk], writes=[ok])
                      else:
                          pr.step("dve", lambda e, psl=psl, osl=osl:
                                  e.tensor_copy(oslab[osl][:, 0:NW],
                                                pj[psl][:]),
                                  reads=[pk], writes=[ok])
                      pr.dma("act",
                             outS[b, jb * 128:(jb + 1) * 128,
                                  ts_ * NW:(ts_ + 1) * NW],
                             oslab[osl][:, 0:NW], reads=[ok],
                             writes=[f"{wnm}S"])

          for dvb in range(max(1, D // VW)):
              wsl = dvb % 2
              pr.dma("sp", wvb[:].rearrange("p (c j) -> p c j", j=VW),
                     wv[:, dvb * VW:(dvb + 1) * VW].rearrange(
                         "(c p) j -> p c j", p=128),
                     writes=["p1wvb"])
              pr.step("dve", lambda e, wsl=wsl: e.tensor_copy(
                  wvbb[wsl][:], wvb[:]),
                  reads=["p1wvb"], writes=[f"p1wvx{wsl}"])
              for tc in range(TC):
                  psl = tc % 2
                  pk = f"p1pj{psl}"
                  for dc in range(DC):
                      pr.step("pe", lambda e, dc=dc, tc=tc, wsl=wsl, psl=psl:
                              e.matmul(
                                  pj[psl][:, 0:VW],
                                  lt[dc][:, tc * 128:(tc + 1) * 128],
                                  wvbb[wsl][:, dc * VW:(dc + 1) * VW],
                                  start=(dc == 0), stop=(dc == DC - 1)),
                              reads=[f"p1wvx{wsl}", f"p1lt{dc}", pk],
                              writes=[pk])
                  osl = tc % 2
                  ok = f"p1os{osl}"
                  pr.step("dve", lambda e, psl=psl, osl=osl: e.tensor_copy(
                      oslab[osl][:, 0:VW], pj[psl][:, 0:VW]),
                      reads=[pk], writes=[ok])
                  pr.dma("act",
                         vS[b, tc * 128:(tc + 1) * 128,
                            dvb * VW:(dvb + 1) * VW],
                         oslab[osl][:, 0:VW], reads=[ok], writes=["vS"])
    def p2_body(b):
        with contextlib.ExitStack() as es2:
          a2 = Alloc(nc, es2)
          sqt = [a2.sb(f"p2_q_b{b}_{s}", [128, TT], BF16) for s in range(2)]
          skt = [a2.sb(f"p2_k_b{b}_{s}", [128, T], BF16) for s in range(2)]
          svt = [a2.sb(f"p2_v_b{b}_{s}", [128, T], BF16) for s in range(2)]
          atb = a2.sb(f"p2_at_b{b}", [128, (T // 128) * TT], BF16)
          mskt = a2.sb(f"p2_msk_b{b}", [128, 128], F32)
          ps_s = [a2.ps(f"p2_ps_b{b}_{s}", [128, TT], F32) for s in range(2)]
          ps_cs = a2.ps(f"p2_cs_b{b}", [128, T // 128], F32)
          cs_sb = a2.sb(f"p2_cssb_b{b}", [128, T // 128], F32)
          sv_sb = a2.sb(f"p2_svsb_b{b}", [128, T // 128 + 1], F32)
          ones_cf = a2.sb(f"p2_onescf_b{b}", [128, 1], BF16)
          rdf = a2.sb(f"p2_rdf_b{b}", [1, TT], F32)
          ps_o = [a2.ps(f"p2_po_b{b}_{s}", [128, TT], F32)
                  for s in range(2)]
          ps_d = a2.ps(f"p2_pd_b{b}", [1, TT], F32)
          ps_b = a2.ps(f"p2_pb_b{b}", [128, TT], F32)
          rd = a2.sb(f"p2_rd_b{b}", [1, TT], F32)
          bcast = a2.sb(f"p2_bc_b{b}", [128, TT], F32)
          aob = [a2.sb(f"p2_ao_b{b}_{s}", [128, TT], BF16) for s in range(2)]
          for h in range(NH):
              ksl = h % 2
              kk, vk = f"p2k{ksl}", f"p2v{ksl}"
              pr.dma("sp", skt[ksl][:], ktS[b, h * 128:(h + 1) * 128, :],
                     writes=[kk])
              pr.dma("sp", svt[ksl][:].rearrange("p (c d) -> p c d", d=128),
                     vS[b, :, h * 128:(h + 1) * 128].rearrange(
                         "(c p) d -> p c d", p=128),
                     writes=[vk])
              if h == 0:
                  pr.step("pool", lambda e: e.memset(ones_cf[:], 1.0),
                          [], ["p2onescf"])
              for sc in range(T // 128):
                  pr.step("pe", lambda e, sc=sc, ksl=ksl: e.matmul(
                      ps_cs[:, sc:sc + 1],
                      svt[ksl][:, sc * 128:(sc + 1) * 128],
                      ones_cf[:], start=True, stop=True),
                      reads=[vk, "p2onescf", "p2cs"], writes=["p2cs"])
              pr.step("dve", lambda e: e.tensor_copy(cs_sb[:], ps_cs[:]),
                      reads=["p2cs"], writes=["p2cssb"])
              pr.step("pool", lambda e: e.memset(
                  sv_sb[:, T // 128:T // 128 + 1], 0.0),
                  [], ["p2svsb"])
              for c in range(T // 128 - 1, -1, -1):
                  pr.step("dve", lambda e, c=c: e.tensor_tensor(
                      sv_sb[:, c:c + 1], sv_sb[:, c + 1:c + 2],
                      cs_sb[:, c:c + 1], OP.add),
                      reads=["p2svsb", "p2cssb"], writes=["p2svsb"])
              for ts_ in range(T // TT):
                  smax = ((ts_ + 1) * TT) // 128
                  qsl = ts_ % 2
                  qk = f"p2q{qsl}"
                  pr.dma("sp", sqt[qsl][:],
                         qtS[b, h * 128:(h + 1) * 128,
                             ts_ * TT:(ts_ + 1) * TT],
                         writes=[qk])
                  for sc in range(smax):
                      psl = sc % 2
                      pk = f"p2ps{psl}"
                      pr.step("pe", lambda e, sc=sc, qsl=qsl, psl=psl,
                              ksl=ksl: e.matmul(
                                  ps_s[psl][:],
                                  skt[ksl][:, sc * 128:(sc + 1) * 128],
                                  sqt[qsl][:], start=True, stop=True),
                              reads=[kk, qk, pk], writes=[pk])
                      dpos = sc - ts_ * SUB
                      if dpos > 0:
                          w_ = min(dpos, SUB) * 128
                          pr.step("pool", lambda e, sc=sc, w_=w_: e.memset(
                              atb[:, sc * TT:sc * TT + w_], 1.0),
                              reads=[], writes=["p2at"])
                      if 0 <= dpos < SUB:
                          pr.step("dve", lambda e, dpos=dpos, psl=psl:
                                  e.tensor_tensor(
                                      mskt[:],
                                      ps_s[psl][:,
                                                dpos * 128:(dpos + 1) * 128],
                                      trimask[:], OP.mult),
                                  reads=[pk, "c_tri"], writes=["p2msk"])
                          pr.step("act", lambda e, sc=sc, dpos=dpos:
                                  e.activation(
                                      atb[:, sc * TT + dpos * 128:
                                          sc * TT + (dpos + 1) * 128],
                                      mskt[:], AF.Exp, scale=1.0),
                                  reads=["p2msk"], writes=["p2at"])
                      lo = max(dpos + 1, 0)
                      if lo < SUB:
                          pr.step("act", lambda e, sc=sc, lo=lo, psl=psl:
                                  e.activation(
                                      atb[:, sc * TT + lo * 128:
                                          sc * TT + SUB * 128],
                                      ps_s[psl][:, lo * 128:SUB * 128],
                                      AF.Exp, scale=1.0),
                                  reads=[pk], writes=["p2at"])
                  osl_ = ts_ % 2
                  pok = f"p2po{osl_}"
                  for sc in range(smax):
                      pr.step("pe", lambda e, sc=sc, ksl=ksl, smax=smax,
                              osl_=osl_: e.matmul(
                                  ps_o[osl_][:],
                                  svt[ksl][:, sc * 128:(sc + 1) * 128],
                                  atb[:, sc * TT:(sc + 1) * TT],
                                  start=(sc == 0), stop=(sc == smax - 1)),
                              reads=[vk, "p2at", pok], writes=[pok])
                  for sc in range(smax):
                      pr.step("pe", lambda e, sc=sc, smax=smax: e.matmul(
                          ps_d[:], ones_c[:],
                          atb[:, sc * TT:(sc + 1) * TT],
                          start=(sc == 0), stop=(sc == smax - 1)),
                          reads=["c_ones", "p2at", "p2pd"], writes=["p2pd"])
                  cnt = float(T - smax * 128)
                  pr.step("dve", lambda e, cnt=cnt: e.tensor_single_scalar(
                      rdf[:], ps_d[:], cnt, OP.add),
                      reads=["p2pd"], writes=["p2rdf"])
                  pr.step("dve", lambda e: e.reciprocal(rd[:], rdf[:]),
                          reads=["p2rdf"], writes=["p2rd"])
                  pr.step("pe", lambda e: e.matmul(
                      ps_b[:], ones_r[:], rd[:], start=True, stop=True),
                      reads=["c_onesr", "p2rd", "p2pb"], writes=["p2pb"])
                  pr.step("dve", lambda e: e.tensor_copy(bcast[:], ps_b[:]),
                          reads=["p2pb"], writes=["p2bc"])
                  asl = ts_ % 2
                  if smax * 128 < T:
                      pr.step("dve", lambda e, asl=asl, smax=smax,
                              osl_=osl_: e.scalar_tensor_tensor(
                                  aob[asl][:], ps_o[osl_][:],
                                  sv_sb[:, smax:smax + 1], bcast[:],
                                  OP.add, OP.mult),
                              reads=[pok, "p2svsb", "p2bc"],
                              writes=[f"p2ao{asl}"])
                  else:
                      pr.step("dve", lambda e, asl=asl, osl_=osl_:
                              e.tensor_tensor(
                                  aob[asl][:], ps_o[osl_][:], bcast[:],
                                  OP.mult),
                              reads=[pok, "p2bc"], writes=[f"p2ao{asl}"])
                  pr.dma("act",
                         aoS[b, h * 128:(h + 1) * 128,
                             ts_ * TT:(ts_ + 1) * TT],
                         aob[asl][:], reads=[f"p2ao{asl}"], writes=["aoS"])
    for b in range(BC):
        if 1 in phases:
            p1_body(b)
        pr.barrier()
        if 2 in phases:
            p2_body(b)
        pr.barrier()

    with contextlib.ExitStack() as es3:
      if 3 in phases:
        a3 = Alloc(nc, es3)
        ngs = max(1, D // 512)
        wob = a3.sb("p3_wo", [128, DC * D], BF16)
        wof = a3.sb("p3_wof", [128, D], F32)
        aosb = [a3.sb(f"p3_ao{s}", [128, DC * TT], BF16) for s in range(2)]
        x0tt = [a3.sb(f"p3_x0{s}", [128, D], F32) for s in range(2)]
        x1t = [a3.sb(f"p3_x1{s}", [128, D], F32) for s in range(2)]
        stats3 = a3.sb("p3_st", [128, 6 * ngs], F32)
        mv3 = a3.sb("p3_mv", [128, 2], F32)
        rstd3 = a3.sb("p3_rs", [128, 1], F32)
        po2 = [a3.ps(f"p3_po{j}", [128, JW], F32) for j in range(D // JW)]
        for dc in range(DC):
            pr.dma("sp", wof[:], wo[dc * 128:(dc + 1) * 128, :],
                   writes=["p3wof"])
            pr.step("dve", lambda e, dc=dc: e.tensor_copy(
                wob[:, dc * D:(dc + 1) * D], wof[:]),
                reads=["p3wof"], writes=["p3wo"])
        for tt in range(NTILE):
            b, lo = divmod(tt * TT, T)
            asl = tt % 2
            aok = f"p3ao{asl}"
            pr.dma("sp", aosb[asl][:].rearrange("p (c t) -> p c t", t=TT),
                   aoS[b, :, lo:lo + TT].rearrange("(c p) t -> p c t",
                                                   p=128),
                   writes=[aok])
            for s4 in range(SUB):
                xsl = s4 % 2
                pr.dma("sp", x0tt[xsl][:],
                       x[b, lo + s4 * 128:lo + (s4 + 1) * 128, :],
                       writes=[f"p3x0{xsl}"])
                for jb in range(D // JW):
                    pk = f"p3po{jb}"
                    for dc in range(DC):
                        pr.step("pe", lambda e, dc=dc, s4=s4, jb=jb, asl=asl:
                                e.matmul(
                                    po2[jb][:],
                                    aosb[asl][:, dc * TT + s4 * 128:
                                              dc * TT + (s4 + 1) * 128],
                                    wob[:, dc * D + jb * JW:
                                        dc * D + (jb + 1) * JW],
                                    start=(dc == 0), stop=(dc == DC - 1)),
                                reads=[aok, "p3wo", pk], writes=[pk])
                emit_ln_stats(pr, [po2[j][:] for j in range(D // JW)],
                              [f"p3po{j}" for j in range(D // JW)],
                              mv3, "p3mv", stats3, "p3st", consts,
                              rstd3, "p3rs")
                for jb in range(D // JW):
                    pr.step("dve", lambda e, jb=jb, xsl=xsl: e.tensor_scalar(
                        x1t[xsl][:, jb * JW:(jb + 1) * JW], po2[jb][:],
                        mv3[:, 0:1], rstd3[:], OP.subtract, OP.mult),
                        reads=[f"p3po{jb}", "p3mv", "p3rs"],
                        writes=[f"p3x1{xsl}"])
                pr.step("dve", lambda e, xsl=xsl: e.tensor_tensor(
                    x1t[xsl][:], x1t[xsl][:], x0tt[xsl][:], OP.add),
                    reads=[f"p3x1{xsl}", f"p3x0{xsl}"], writes=[f"p3x1{xsl}"])
                pr.dma("act", x1S[b, lo + s4 * 128:lo + (s4 + 1) * 128, :],
                       x1t[xsl][:], reads=[f"p3x1{xsl}"], writes=["x1S"])
    pr.barrier()

    ICN = I // IW
    NPART = getattr(cfg, 'NPART_OVR', 8)
    ICP = ICN // NPART
    assert ICN % NPART == 0 and (I // NPART) % QW == 0
    HB = I // NPART
    with contextlib.ExitStack() as es4:
      if 4 in phases:
        a4 = Alloc(nc, es4)
        ngs = max(1, D // 512)
        qp4 = QuantPool(a4, "p4", QW, cfg.QSLOTS)
        x1in = [a4.sb(f"p4_x1{s}", [128, D], F32) for s in range(2)]
        stats4 = a4.sb("p4_st", [128, 6 * ngs], F32)
        mv4 = a4.sb("p4_mv", [128, 2], F32)
        rstd4 = a4.sb("p4_rs", [128, 1], F32)
        hdq = [a4.sb(f"p4_hdq{s}", [128, QW], BF16) for s in range(3)]
        hdqT = [a4.sb(f"p4_hT{ic}", [128, TT], BF16) for ic in range(I // 128)]
        xdqT = [a4.sb(f"p4_xT{dc}", [128, TT], BF16) for dc in range(DC)]
        ptr4 = [a4.ps(f"p4_pt{s}", [128, 512], BF16) for s in range(2)]
        ps_g = a4.ps("p4_pg", [128, IW], F32)
        ps_u = a4.ps("p4_pu", [128, IW], F32)
        ps_dn = [a4.ps(f"p4_pd{k}", [128, HW_], F32) for k in range(SUB)]

        def p4_tile(tt):
            b, lo = divmod(tt * TT, T)
            pr.cur_tag = f'P4t{tt}'
            with contextlib.ExitStack() as esp:
                ap_ = Alloc(nc, esp)
                l2t = ap_.sb(f"p4_l2_t{tt}", [128, D], F32)
                xdq = [ap_.sb(f"p4_xdq_t{tt}_{s}", [128, D], BF16)
                       for s in range(2)]
                sched4 = QuantSched(pr, qp4, consts)
                for s4 in range(SUB):
                    xsl = s4 % 2
                    xk = f"p4x1{xsl}"
                    pr.dma("sp", x1in[xsl][:],
                           x1S[b, lo + s4 * 128:lo + (s4 + 1) * 128, :],
                           writes=[xk])
                    srcs = ([x1in[xsl][:, i * 512:(i + 1) * 512]
                             for i in range(ngs)] if D >= 512
                            else [x1in[xsl][:, 0:D]])
                    emit_ln_stats(pr, srcs, [xk] * ngs, mv4, "p4mv", stats4,
                                  "p4st", consts, rstd4, "p4rs")
                    pr.step("dve", lambda e, xsl=xsl: e.tensor_scalar(
                        l2t[:], x1in[xsl][:], mv4[:, 0:1], rstd4[:],
                        OP.subtract, OP.mult),
                        reads=[xk, "p4mv", "p4rs"], writes=["p4l2"])

                    def x_post(s4=s4, xsl=xsl):
                        xq = f"p4xdq{xsl}"
                        for dc in range(DC):
                            psl = dc % 2
                            pr.step("pe", lambda e, dc=dc, xsl=xsl, psl=psl:
                                    e.transpose(
                                        ptr4[psl][:, 0:128],
                                        xdq[xsl][:, dc * 128:(dc + 1) * 128],
                                        ident[:]),
                                    reads=[xq, "c_ident"],
                                    writes=[f"p4pt{psl}"])
                            pr.step("dve", lambda e, dc=dc, s4=s4, psl=psl:
                                    e.tensor_copy(
                                        xdqT[dc][:, s4 * 128:(s4 + 1) * 128],
                                        ptr4[psl][:, 0:128]),
                                    reads=[f"p4pt{psl}"], writes=[f"p4xT{dc}"])

                    nq = max(1, D // QW)
                    for q in range(nq):
                        sched4.push(l2t[:, q * QW:(q + 1) * QW], "p4l2",
                                    xdq[xsl][:, q * QW:(q + 1) * QW],
                                    f"p4xdq{xsl}",
                                    post=x_post if q == nq - 1 else None)
                sched4.flush()
            pr.barrier()
            with contextlib.ExitStack() as esm:
                am = Alloc(nc, esm)
                sched4m = QuantSched(pr, qp4, consts)
                hq_counter = [0]
                g_rhs = [am.sb(f"p4_gr_t{tt}_{s}", [128, DC * IW], BF16)
                         for s in range(2)]
                u_rhs = [am.sb(f"p4_ur_t{tt}_{s}", [128, DC * IW], BF16)
                         for s in range(2)]
                sg = [am.sb(f"p4_sg_t{tt}_{s}", [128, IW], BF16) for s in range(2)]
                hbuf = [[am.sb(f"p4_h_t{tt}_{p}_{k}", [128, HB], BF16)
                         for k in range(SUB)] for p in range(2)]
                for part in range(NPART):
                    pp = part % 2
                    for ici in range(ICP):
                        ic = part * ICP + ici
                        wsl = ic % 2
                        gk, uk = f"p4gr{wsl}", f"p4ur{wsl}"
                        pr.dma("sp", g_rhs[wsl][:].rearrange(
                                   "p (c i) -> p c i", i=IW),
                               gwTS[:, ic * IW:(ic + 1) * IW].rearrange(
                                   "(c p) i -> p c i", p=128),
                               writes=[gk])
                        pr.dma("sp", u_rhs[wsl][:].rearrange(
                                   "p (c i) -> p c i", i=IW),
                               upTS[:, ic * IW:(ic + 1) * IW].rearrange(
                                   "(c p) i -> p c i", p=128),
                               writes=[uk])
                        for s4 in range(SUB):
                            hk = f"p4h{pp}_{s4}"
                            gp = s4 % 2
                            pg = ps_g if gp == 0 else ps_dn[0]
                            pu = ps_u if gp == 0 else ps_dn[1]
                            pgk, puk = f"p4pg{gp}", f"p4pu{gp}"
                            sgk = f"p4sg{gp}"
                            for dc in range(DC):
                                pr.step("pe", lambda e, dc=dc, s4=s4, wsl=wsl,
                                        pg=pg: e.matmul(
                                            pg[:, 0:IW],
                                            xdqT[dc][:,
                                                     s4 * 128:(s4 + 1) * 128],
                                            g_rhs[wsl][:,
                                                       dc * IW:(dc + 1) * IW],
                                            start=(dc == 0),
                                            stop=(dc == DC - 1)),
                                        reads=[f"p4xT{dc}", gk, pgk],
                                        writes=[pgk])
                            for dc in range(DC):
                                pr.step("pe", lambda e, dc=dc, s4=s4, wsl=wsl,
                                        pu=pu: e.matmul(
                                            pu[:, 0:IW],
                                            xdqT[dc][:,
                                                     s4 * 128:(s4 + 1) * 128],
                                            u_rhs[wsl][:,
                                                       dc * IW:(dc + 1) * IW],
                                            start=(dc == 0),
                                            stop=(dc == DC - 1)),
                                        reads=[f"p4xT{dc}", uk, puk],
                                        writes=[puk])
                            pr.step("act", lambda e, pg=pg, gp=gp:
                                    e.activation(sg[gp][:], pg[:, 0:IW],
                                                 AF.Silu),
                                    reads=[pgk], writes=[sgk])
                            pr.step("dve", lambda e, pp=pp, s4=s4, ici=ici,
                                    pu=pu, gp=gp: e.tensor_tensor(
                                        hbuf[pp][s4][:,
                                                     ici * IW:(ici + 1) * IW],
                                        pu[:, 0:IW], sg[gp][:], OP.mult),
                                    reads=[puk, sgk], writes=[hk])
                    for s4 in range(SUB):
                        hk = f"p4h{pp}_{s4}"
                        for q5 in range(HB // QW):
                            hsl = hq_counter[0] % 3
                            hq_counter[0] += 1
                            hqk = f"p4hdq{hsl}"

                            def h_post(s4=s4, q5=q5, hsl=hsl, hqk=hqk,
                                       part=part):
                                for j4 in range(QW // 128):
                                    psl = j4 % 2
                                    icc = (part * HB // 128
                                           + q5 * (QW // 128) + j4)
                                    pr.step("pe", lambda e, j4=j4, hsl=hsl,
                                            psl=psl: e.transpose(
                                                ptr4[psl][:, 0:128],
                                                hdq[hsl][:, j4 * 128:
                                                          (j4 + 1) * 128],
                                                ident[:]),
                                            reads=[hqk, "c_ident"],
                                            writes=[f"p4pt{psl}"])
                                    pr.step("dve", lambda e, icc=icc, s4=s4,
                                            psl=psl: e.tensor_copy(
                                                hdqT[icc][:, s4 * 128:
                                                          (s4 + 1) * 128],
                                                ptr4[psl][:, 0:128]),
                                            reads=[f"p4pt{psl}"],
                                            writes=[f"p4hT{icc}"])

                            sched4m.push(
                                hbuf[pp][s4][:, q5 * QW:(q5 + 1) * QW],
                                hk, hdq[hsl][:], hqk, post=h_post)
                sched4m.flush()
            pr.barrier()
            with contextlib.ExitStack() as esc:
                ac = Alloc(nc, esc)
                d_rhs = [ac.sb(f"p4_dr_t{tt}_{s}", [128, HW_], BF16)
                         for s in range(2)]
                fo = [ac.sb(f"p4_fo_t{tt}_{k}", [128, D], F32) for k in range(SUB)]
                outt = ac.sb(f"p4_out_t{tt}", [128, D], F32)
                for hb in range(D // HW_):
                    for icc in range(I // 128):
                        dsl = icc % 2
                        dk_ = f"p4dr{dsl}"
                        pr.dma("sp", d_rhs[dsl][:],
                               dwTS[icc * 128:(icc + 1) * 128,
                                    hb * HW_:(hb + 1) * HW_],
                               writes=[dk_])
                        for s4 in range(SUB):
                            pr.step("pe", lambda e, icc=icc, s4=s4, dsl=dsl:
                                    e.matmul(
                                        ps_dn[s4][:],
                                        hdqT[icc][:, s4 * 128:(s4 + 1) * 128],
                                        d_rhs[dsl][:],
                                        start=(icc == 0),
                                        stop=(icc == I // 128 - 1)),
                                    reads=[f"p4hT{icc}", dk_, f"p4pd{s4}"],
                                    writes=[f"p4pd{s4}"])
                    for s4 in range(SUB):
                        pr.step("dve", lambda e, s4=s4, hb=hb: e.tensor_copy(
                            fo[s4][:, hb * HW_:(hb + 1) * HW_], ps_dn[s4][:]),
                            reads=[f"p4pd{s4}"], writes=[f"p4fo{s4}"])
                for s4 in range(SUB):
                    xsl = s4 % 2
                    xk = f"p4x1{xsl}"
                    pr.dma("sp", x1in[xsl][:],
                           x1S[b, lo + s4 * 128:lo + (s4 + 1) * 128, :],
                           writes=[xk])
                    srcs = ([fo[s4][:, i * 512:(i + 1) * 512]
                             for i in range(ngs)] if D >= 512
                            else [fo[s4][:, 0:D]])
                    emit_ln_stats(pr, srcs, [f"p4fo{s4}"] * ngs, mv4, "p4mv",
                                  stats4, "p4st", consts, rstd4, "p4rs")
                    pr.step("dve", lambda e, s4=s4: e.tensor_scalar(
                        outt[:], fo[s4][:], mv4[:, 0:1], rstd4[:],
                        OP.subtract, OP.mult),
                        reads=[f"p4fo{s4}", "p4mv", "p4rs"], writes=["p4out"])
                    pr.step("dve", lambda e, xsl=xsl: e.tensor_tensor(
                        outt[:], outt[:], x1in[xsl][:], OP.add),
                        reads=["p4out", xk], writes=["p4out"])
                    pr.dma("act",
                           out[b, lo + s4 * 128:lo + (s4 + 1) * 128, :],
                           outt[:], reads=["p4out"], writes=["outD"])
            pr.barrier()

        for tt in range(NTILE):
            p4_tile(tt)
    es_top.close()

    with contextlib.ExitStack() as esf:
        sems = {e: esf.enter_context(nc.semaphore(f"sem_{e}"))
                for e in ENGINES}
        for q in ("sp", "act", "pool"):
            for ln in range(Prog.NLANES):
                sems[(q, ln)] = esf.enter_context(
                    nc.semaphore(f"sem_{q}{ln}"))
        blk = esf.enter_context(nc.Block())
        pr.emit(blk, sems)
    return nc


from concourse.bass_utils import run_bass_kernel_spmd

_NC_CACHE = {}
RELAX = False


def kernel(L0_attn_x, L0_attn_Wq, L0_attn_Wk, L0_attn_Wv, L0_attn_Wo,
           L0_ffn_gate_w, L0_ffn_up_w, L0_ffn_down_w):
    x = np.asarray(L0_attn_x)
    B, T, D = x.shape
    n_cores = 8
    BC = B // n_cores
    cfg = Cfg(T=T, D=D, I=np.asarray(L0_ffn_gate_w).shape[0], BC=BC)
    key = (T, D, cfg.I, BC)
    if key not in _NC_CACHE:
        _NC_CACHE[key] = build_mk(cfg, relax=RELAX)
    nc = _NC_CACHE[key]
    w = {
        "wq": np.ascontiguousarray(L0_attn_Wq),
        "wk": np.ascontiguousarray(L0_attn_Wk),
        "wv": np.ascontiguousarray(L0_attn_Wv),
        "wo": np.ascontiguousarray(L0_attn_Wo),
        "gw": np.ascontiguousarray(L0_ffn_gate_w),
        "uw": np.ascontiguousarray(L0_ffn_up_w),
        "dw": np.ascontiguousarray(L0_ffn_down_w),
    }
    in_maps = [dict(w, x=np.ascontiguousarray(x[c * BC:(c + 1) * BC]))
               for c in range(n_cores)]
    res = run_bass_kernel_spmd(nc, in_maps, core_ids=list(range(n_cores)))
    out = np.concatenate([np.asarray(r["out"]).reshape(BC, T, D)
                          for r in res.results], axis=0)
    return out.astype(np.float32)
